# version 01 (final)
# baseline (speedup 1.0000x reference)
import numpy as np
import ml_dtypes
from contextlib import ExitStack
import concourse.bass as bass
import concourse.mybir as mybir
from concourse.bass_utils import run_bass_kernel_spmd

F32 = mybir.dt.float32
BF16 = mybir.dt.bfloat16
I32 = mybir.dt.int32
AF = mybir.ActivationFunctionType
OP = mybir.AluOpType
bf16 = ml_dtypes.bfloat16

NCORES = 8
f32 = np.float32


class Ctx:

    def __init__(self, nc):
        self.nc = nc
        self.es = ExitStack()
        self.sems = {}
        self.cnt = {}

    def sbuf(self, name, shape, dt=F32):
        return self.es.enter_context(self.nc.sbuf_tensor(name, shape, dt))

    def psum(self, name, shape, dt=F32):
        return self.es.enter_context(self.nc.psum_tensor(name, shape, dt))

    def sem(self, name):
        if name not in self.sems:
            self.sems[name] = self.es.enter_context(self.nc.semaphore(name))
            self.cnt[name] = 0
        return self.sems[name]

    def inc(self, instr, name, n=1):
        instr.then_inc(self.sems[name], n)
        self.cnt[name] += n
        return self.cnt[name]

    def fence(self, eng, name, instr):
        self.sem(name)
        self.inc(instr, name, 1)
        eng.wait_ge(self.sems[name], self.cnt[name])


class PStream:

    def __init__(self, c, name, per=1):
        self.c = c
        self.name = name
        self.per = per
        self.sems = [c.sem(name + "0"), c.sem(name + "1")]

    def inc(self, instr, i):
        self.c.inc(instr, self.name + str(i % 2), 16)

    def wait(self, eng, i):
        eng.wait_ge(self.sems[i % 2], 16 * self.per * (i // 2 + 1))


class LNShared:

    def __init__(self, c, ncols, use_tmp=True):
        self.c = c
        self.ncols = ncols
        self.r_bf = c.sbuf("ln_rbf", [128, 2, ncols], BF16)
        self.r2_bf = c.sbuf("ln_r2bf", [128, 2, ncols], BF16)
        self.rows = c.sbuf("ln_rows", [1, 4, ncols], F32)
        self.tmp = c.sbuf("ln_tmp", [128, 2, ncols], F32) if use_tmp else None
        self.ones = c.sbuf("ln_ones", [128, 1], BF16)
        self.ps_s = c.psum("ln_pss", [1, ncols], F32)
        self.ps_q = c.psum("ln_psq", [1, ncols], F32)
        self.eps = np.float32(1e-5)


class LNEmitter:

    def __init__(self, c, sh, prefix, src, dst, nmt, prev=None,
                 gamma=None, beta=None, bcast=None,
                 src_ready=None, dst_free=None, ndim=2048.0,
                 dst_slots=None, bc_free=None):
        self.c, self.sh, self.p = c, sh, prefix
        self.src, self.dst, self.nmt = src, dst, nmt
        self.gamma, self.beta = gamma, beta
        self.bcast = bcast
        self.src_ready = src_ready or (lambda e, mt: None)
        self.dst_free = dst_free or (lambda e, mt: None)
        self.prev = prev
        self.ndim = ndim
        self.dst_slots = dst_slots
        self.bc_free = bc_free or (lambda e: None)

        for s in ["_st", "_sq", "_smm", "_mm", "_var", "_row", "_bc", "_ap", "_fin"]:
            c.sem(prefix + s)

    def dtile(self, mt):
        return self.dst[:, mt % self.dst_slots, :] if self.dst_slots else self.dst[:, mt, :]

    def emit(self, name, e):
        c, sh, p = self.c, self.sh, self.p
        S = c.sems
        nmt = self.nmt
        ncols = sh.ncols
        nh = ceil_div(ncols, 512)
        if name == "vector":
            for mt in range(nmt):
                self.src_ready(e, mt)
                if mt >= 2:
                    e.wait_ge(S[p + "_smm"], mt - 1)
                c.inc(e.tensor_copy(sh.r_bf[:, mt % 2, :], self.src[:, mt, :]), p + "_st", 1)
            e.wait_ge(S[p + "_mm"], 1)
            c.fence(e, "v_o", e.tensor_scalar_mul(sh.rows[:, 0, :], sh.ps_s[:, :], 1.0 / self.ndim))
            c.fence(e, "v_o", e.tensor_scalar_mul(sh.rows[:, 2, :], sh.ps_q[:, :], 1.0 / self.ndim))
            c.fence(e, "v_o", e.tensor_mul(sh.rows[:, 3, :], sh.rows[:, 0, :], sh.rows[:, 0, :]))
            ins = e.tensor_sub(sh.rows[:, 3, :], sh.rows[:, 2, :], sh.rows[:, 3, :])
            c.inc(ins, p + "_var", 1)
            for mt in range(nmt):
                e.wait_ge(S[p + "_bc"], 128)
                if self.gamma is None:
                    self.dst_free(e, mt)
                    c.fence(e, "v_o", e.tensor_sub(self.dtile(mt), self.src[:, mt, :], self.bcast[:, 0, :]))
                    ins = e.tensor_mul(self.dtile(mt), self.dtile(mt), self.bcast[:, 1, :])
                    c.inc(ins, p + "_fin", 1)
                else:
                    c.fence(e, "v_o", e.tensor_sub(sh.tmp[:, mt % 2, :], self.src[:, mt, :], self.bcast[:, 0, :]))
                    c.fence(e, "v_o", e.tensor_mul(sh.tmp[:, mt % 2, :], sh.tmp[:, mt % 2, :], self.bcast[:, 1, :]))
                    self.dst_free(e, mt)
                    ins = e.tensor_scalar(self.dtile(mt), sh.tmp[:, mt % 2, :],
                                          self.gamma[:, mt:mt + 1], self.beta[:, mt:mt + 1],
                                          op0=OP.mult, op1=OP.add)
                    c.inc(ins, p + "_fin", 1)
        elif name == "scalar":
            for mt in range(nmt):
                e.wait_ge(S[p + "_st"], mt + 1)
                c.inc(e.activation(out=sh.r2_bf[:, mt % 2, :], in_=sh.r_bf[:, mt % 2, :],
                                   func=AF.Square), p + "_sq", 1)
            e.wait_ge(S[p + "_var"], 1)
            c.fence(e, "se_o", e.activation(out=sh.rows[:, 1, :], in_=sh.rows[:, 3, :],
                                            func=AF.Ln, bias=sh.eps))
            ins = e.activation(out=sh.rows[:, 1, :], in_=sh.rows[:, 1, :], func=AF.Exp,
                               scale=np.float32(-0.5))
            c.inc(ins, p + "_row", 1)
        elif name == "tensor":
            if self.prev is not None:
                e.wait_ge(S[self.prev.p + "_var"], 1)
                e.wait_ge(S[self.prev.p + "_bc"], 128)
            for mt in range(nmt):
                e.wait_ge(S[p + "_sq"], mt + 1)
                for n in range(nh):
                    sl = bass.ts(n, 512)
                    e.matmul(sh.ps_s[:, sl], sh.ones[:, :], sh.r_bf[:, mt % 2, sl],
                             start=(mt == 0), stop=(mt == nmt - 1))
                    e.matmul(sh.ps_q[:, sl], sh.ones[:, :], sh.r2_bf[:, mt % 2, sl],
                             start=(mt == 0), stop=(mt == nmt - 1))
                c.inc(e.drain(), p + "_smm", 1)
            c.inc(e.drain(), p + "_mm", 1)
        elif name == "gpsimd":
            e.wait_ge(S[p + "_row"], 1)
            self.bc_free(e)
            e.dma_start(out=self.bcast[0:1, 0:2, :], in_=sh.rows[:, 0:2, :]).then_inc(S[p + "_bc"], 16)
            c.cnt[p + "_bc"] += 16
            pp = 1
            while pp < 128:
                e.wait_ge(S[p + "_bc"], c.cnt[p + "_bc"])
                e.dma_start(out=self.bcast[pp:2 * pp, :, :], in_=self.bcast[0:pp, :, :]).then_inc(S[p + "_bc"], 16)
                c.cnt[p + "_bc"] += 16
                pp *= 2


def ceil_div(a, b):
    return (a + b - 1) // b


def np_bf16(x):
    return np.ascontiguousarray(x.astype(bf16))



def build_s1():
    nc = bass.Bass("TRN2", target_bir_lowering=False, debug=False)
    xT = nc.dram_tensor("xT", [2048, 8192], F32, kind="ExternalInput").ap()
    WinS = nc.dram_tensor("WinS", [2048, 1024], F32, kind="ExternalInput").ap()
    convw = nc.dram_tensor("convw", [128, 4, 3], F32, kind="ExternalInput").ap()
    convb = nc.dram_tensor("convb", [128, 4, 1], F32, kind="ExternalInput").ap()
    xcT = nc.dram_tensor("xcT", [512, 8192], BF16, kind="ExternalOutput").ap()
    zsT = nc.dram_tensor("zsT", [512, 8192], BF16, kind="ExternalOutput").ap()
    xpT = nc.dram_tensor("xpT", [512, 8192], F32, kind="Internal").ap()

    c = Ctx(nc)
    NCH = 16
    NB = 4 * NCH
    with c.es:
        winbf = c.sbuf("winbf", [128, 16, 8, 128], BF16)
        xn = [c.sbuf(f"xn{i}", [128, 16, 512], BF16) for i in range(2)]
        xp_stage = [c.sbuf(f"xps{i}", [128, 4, 512], F32) for i in range(2)]
        zs_stage = [c.sbuf(f"zss{i}", [128, 4, 512], BF16) for i in range(2)]
        cw = c.sbuf("cw", [128, 4, 3], F32)
        cb = c.sbuf("cb", [128, 4, 1], F32)
        wbuf = [c.sbuf(f"wbuf{i}", [128, 514], F32) for i in range(2)]
        tmp1 = c.sbuf("tmp1", [128, 512], F32)
        tmp2 = c.sbuf("tmp2", [128, 512], F32)
        xc_stage = [c.sbuf(f"xcs{i}", [128, 512], BF16) for i in range(2)]
        ps = [c.psum(f"ps{i}", [128, 512], F32) for i in range(8)]

        for s in ["s_w", "s_xn", "s_mm", "s_pt", "s_evv", "s_evs", "s_oxp", "s_ozs",
                  "s_bw", "s_b1", "s_b2", "s_b3", "s_bo"]:
            c.sem(s)
        SEM = c.sems

        def b_window(i):
            mt, ch = divmod(i, NCH)
            t0 = ch * 512
            lo = t0 - 1 if t0 % 4096 != 0 else t0
            hi = t0 + 513 if (t0 + 512) % 4096 != 0 else t0 + 512
            return mt, ch, t0, lo, hi

        with nc.Block() as block:
            @block.gpsimd
            def _(g):
                g.dma_start(out=winbf[:, :, :, :],
                            in_=WinS.rearrange("(kt p) (mt m) -> p kt mt m", p=128, m=128)
                            ).then_inc(SEM["s_w"], 16)
                g.dma_start(out=cw[:, :, :], in_=convw[:, :, :]).then_inc(SEM["s_w"], 16)
                g.dma_start(out=cb[:, :, :], in_=convb[:, :, :]).then_inc(SEM["s_w"], 16)
                for ch in range(NCH):
                    t0 = ch * 512
                    if ch >= 2:
                        g.wait_ge(SEM["s_mm"], ch - 1)
                    g.dma_start(out=xn[ch % 2][:, :, :],
                                in_=xT[:, t0:t0 + 512].rearrange("(kt p) n -> p kt n", p=128)
                                ).then_inc(SEM["s_xn"], 16)
                for i in range(NB + 2):
                    if i < NB:
                        mt, ch, t0, lo, hi = b_window(i)
                        if i >= 2:
                            g.wait_ge(SEM["s_b2"], i - 1)
                        if i == 0:
                            g.wait_ge(SEM["s_oxp"], 16 * NCH)
                        if lo == t0:
                            g.memset(wbuf[i % 2][:, 0:1], 0.0)
                        if hi == t0 + 512:
                            g.memset(wbuf[i % 2][:, 513:514], 0.0)
                        g.dma_start(out=wbuf[i % 2][:, lo - (t0 - 1):hi - (t0 - 1)],
                                    in_=xpT[mt * 128:(mt + 1) * 128, lo:hi]).then_inc(SEM["s_bw"], 16)
                    if i >= 2:
                        j = i - 2
                        mt, ch, t0, lo, hi = b_window(j)
                        g.wait_ge(SEM["s_b3"], j + 1)
                        g.dma_start(out=xcT[mt * 128:(mt + 1) * 128, t0:t0 + 512],
                                    in_=xc_stage[j % 2][:, :]).then_inc(SEM["s_bo"], 16)
                g.wait_ge(SEM["s_bo"], 16 * NB)

            @block.tensor
            def _(t):
                t.wait_ge(SEM["s_w"], 48)
                for ch in range(NCH):
                    t.wait_ge(SEM["s_xn"], 16 * (ch + 1))
                    for mt in range(8):
                        if ch >= 1:
                            if mt < 4:
                                t.wait_ge(SEM["s_evv"], 4 * (ch - 1) + mt + 1)
                            else:
                                t.wait_ge(SEM["s_evs"], 4 * (ch - 1) + (mt - 4) + 1)
                        for kt in range(16):
                            ins = t.matmul(ps[mt][:, :], winbf[:, kt, mt, :], xn[ch % 2][:, kt, :],
                                           start=(kt == 0), stop=(kt == 15))
                        c.inc(ins, "s_pt", 1)
                        if mt == 7:
                            c.inc(t.drain(), "s_mm", 1)

            @block.vector
            def _(v):
                for ch in range(NCH):
                    for mt in range(4):
                        v.wait_ge(SEM["s_pt"], 8 * ch + mt + 1)
                        if ch >= 2:
                            v.wait_ge(SEM["s_oxp"], 16 * (ch - 1))
                        c.inc(v.tensor_copy(xp_stage[ch % 2][:, mt, :], ps[mt][:, :]), "s_evv", 1)
                for i in range(NB):
                    v.wait_ge(SEM["s_b1"], i + 1)
                    par = i % 2
                    v.scalar_tensor_tensor(out=tmp2[:, :], in0=wbuf[par][:, 0:512],
                                           scalar=cw[:, i // NCH, 0:1], in1=tmp1[:, :],
                                           op0=OP.mult, op1=OP.add)
                    c.inc(v.scalar_tensor_tensor(out=tmp2[:, :], in0=wbuf[par][:, 2:514],
                                                 scalar=cw[:, i // NCH, 2:3], in1=tmp2[:, :],
                                                 op0=OP.mult, op1=OP.add), "s_b2", 1)

            @block.scalar
            def _(sc):
                for ch in range(NCH):
                    for mt in range(4, 8):
                        sc.wait_ge(SEM["s_pt"], 8 * ch + mt + 1)
                        if ch >= 2:
                            sc.wait_ge(SEM["s_ozs"], 16 * (ch - 1))
                        c.inc(sc.activation(out=zs_stage[ch % 2][:, mt - 4, :], in_=ps[mt][:, :],
                                            func=AF.Silu), "s_evs", 1)
                for i in range(NB):
                    mt = i // NCH
                    sc.wait_ge(SEM["s_bw"], 16 * (i + 1))
                    c.inc(sc.activation(out=tmp1[:, :], in_=wbuf[i % 2][:, 1:513],
                                        func=AF.Copy, scale=cw[:, mt, 1:2]), "s_b1", 1)
                    sc.wait_ge(SEM["s_b2"], i + 1)
                    if i >= 2:
                        sc.wait_ge(SEM["s_bo"], 16 * (i - 1))
                    c.inc(sc.activation(out=xc_stage[i % 2][:, :], in_=tmp2[:, :],
                                        func=AF.Silu, bias=cb[:, mt, 0:1]), "s_b3", 1)

            @block.sync
            def _(s):
                for ch in range(NCH):
                    t0 = ch * 512
                    s.wait_ge(SEM["s_evv"], 4 * (ch + 1))
                    s.dma_start(out=xpT.rearrange("(mt p) t -> p mt t", p=128)[:, :, t0:t0 + 512],
                                in_=xp_stage[ch % 2][:, :, :]).then_inc(SEM["s_oxp"], 16)
                    s.wait_ge(SEM["s_evs"], 4 * (ch + 1))
                    s.dma_start(out=zsT.rearrange("(mt p) t -> p mt t", p=128)[:, :, t0:t0 + 512],
                                in_=zs_stage[ch % 2][:, :, :]).then_inc(SEM["s_ozs"], 16)
                s.wait_ge(SEM["s_ozs"], 16 * NCH)
    return nc


def stage_s1(ins, core):
    x = ins["L0_attn_x"]
    B, L, H = x.shape
    xT = np.ascontiguousarray(x.reshape(B * L, H).T)
    W_in = ins["L0_attn_W_in"]
    lo = 512 * core
    WinS = np.ascontiguousarray(np.concatenate(
        [W_in[:, lo:lo + 512], W_in[:, 4096 + lo:4096 + lo + 512]], axis=1))
    convw = np.ascontiguousarray(ins["L0_attn_conv_w"][lo:lo + 512, 0, :].reshape(4, 128, 3).transpose(1, 0, 2))
    convb = np.ascontiguousarray(ins["L0_attn_conv_b"][lo:lo + 512].reshape(4, 128, 1).transpose(1, 0, 2))
    return {"xT": xT, "WinS": WinS, "convw": convw, "convb": convb}





def build_s2():
    nc = bass.Bass("TRN2", target_bir_lowering=False, debug=False)
    xcT = nc.dram_tensor("xcT_t", [4096, 1024], BF16, kind="ExternalInput").ap()
    Wx = nc.dram_tensor("Wx", [128, 32, 160], F32, kind="ExternalInput").ap()
    Wdt = nc.dram_tensor("Wdt", [128, 4096], F32, kind="ExternalInput").ap()
    bdt = nc.dram_tensor("bdt", [128, 32], F32, kind="ExternalInput").ap()
    dtT = nc.dram_tensor("dtT_t", [4096, 1024], BF16, kind="ExternalOutput").ap()
    BCt = nc.dram_tensor("BCt", [32, 1024], F32, kind="ExternalOutput").ap()

    c = Ctx(nc)
    with c.es:
        xc = c.sbuf("xc", [128, 32, 1024], BF16)
        wx = c.sbuf("wx", [128, 32, 160], BF16)
        wdt = c.sbuf("wdt", [128, 32, 128], BF16)
        bdt_t = c.sbuf("bdt_t", [128, 32], F32)
        dtin = c.sbuf("dtin", [128, 1024], BF16)
        bc_s = c.sbuf("bc_s", [32, 1024], F32)
        xw = [c.sbuf(f"xw{i}", [128, 512], F32) for i in range(2)]
        sp_t1 = [c.sbuf(f"spt1_{i}", [128, 512], F32) for i in range(2)]
        sp_t4 = c.sbuf("sp_t4", [128, 512], F32)
        dt_stage = [c.sbuf(f"dts{i}", [128, 512], BF16) for i in range(2)]
        ps_a = c.psum("ps_a", [128, 512], F32)
        ps_b = c.psum("ps_b", [32, 512], F32)
        ps_d = [c.psum(f"ps_d{i}", [128, 512], F32) for i in range(2)]

        for s in ["s_ld", "s_mm", "s_ev", "s_sp1", "s_sp4", "s_dt", "s_out"]:
            c.sem(s)
        SEM = c.sems

        with nc.Block() as block:
            @block.gpsimd
            def _(g):
                g.dma_start(out=xc[:, :, :],
                            in_=xcT.rearrange("(kt p) t -> p kt t", p=128)).then_inc(SEM["s_ld"], 16)
                g.dma_start(out=wx[:, :, :], in_=Wx[:, :, :]).then_inc(SEM["s_ld"], 16)
                g.dma_start(out=wdt[:, :, :],
                            in_=Wdt.rearrange("p (mt m) -> p mt m", m=128)).then_inc(SEM["s_ld"], 16)
                g.dma_start(out=bdt_t[:, :], in_=bdt[:, :]).then_inc(SEM["s_ld"], 16)

            @block.tensor
            def _(t):
                t.wait_ge(SEM["s_ld"], 64)
                for n in range(2):
                    if n >= 1:
                        t.wait_ge(SEM["s_ev"], 2 * (n - 1) + 1)
                    for kt in range(32):
                        t.matmul(ps_a[:, :], wx[:, kt, 0:128], xc[:, kt, bass.ts(n, 512)],
                                 start=(kt == 0), stop=(kt == 31))
                    c.inc(t.drain(), "s_mm", 1)
                    if n >= 1:
                        t.wait_ge(SEM["s_ev"], 2 * (n - 1) + 2)
                    for kt in range(32):
                        t.matmul(ps_b[:, :], wx[:, kt, 128:160], xc[:, kt, bass.ts(n, 512)],
                                 start=(kt == 0), stop=(kt == 31))
                    c.inc(t.drain(), "s_mm", 1)
                t.wait_ge(SEM["s_ev"], 4)
                for mt in range(32):
                    for n in range(2):
                        i = mt * 2 + n
                        if i >= 2:
                            t.wait_ge(SEM["s_dt"], i - 1)
                        t.matmul(ps_d[i % 2][:, :], wdt[:, mt, :], dtin[:, bass.ts(n, 512)],
                                 start=True, stop=True)
                        c.inc(t.drain(), "s_mm", 1)

            @block.vector
            def _(v):
                for n in range(2):
                    v.wait_ge(SEM["s_mm"], 2 * n + 1)
                    c.inc(v.tensor_copy(dtin[:, bass.ts(n, 512)], ps_a[:, :]), "s_ev", 1)
                    v.wait_ge(SEM["s_mm"], 2 * n + 2)
                    c.inc(v.tensor_copy(bc_s[:, bass.ts(n, 512)], ps_b[:, :]), "s_ev", 1)
                for i in range(64):
                    mt = i // 2
                    v.wait_ge(SEM["s_sp1"], i + 1)
                    v.tensor_scalar(sp_t4[:, :], xw[i % 2][:, :], bdt_t[:, mt:mt + 1], 0.0,
                                    op0=OP.add, op1=OP.max)
                    v.wait_ge(SEM["s_sp4"], i + 1)
                    if i >= 2:
                        v.wait_ge(SEM["s_out"], 16 * (i - 1))
                    c.inc(v.tensor_add(dt_stage[i % 2][:, :], sp_t1[i % 2][:, :], sp_t4[:, :]), "s_dt", 1)

            @block.scalar
            def _(sc):
                for i in range(64):
                    mt = i // 2
                    sc.wait_ge(SEM["s_mm"], 4 + i + 1)
                    if i >= 2:
                        sc.wait_ge(SEM["s_dt"], i - 1)
                    c.inc(sc.copy(xw[i % 2][:, :], ps_d[i % 2][:, :]), "s_sp1", 1)
                    sc.activation(out=sp_t1[i % 2][:, :], in_=xw[i % 2][:, :], func=AF.Abs,
                                  bias=bdt_t[:, mt:mt + 1])
                    sc.activation(out=sp_t1[i % 2][:, :], in_=sp_t1[i % 2][:, :], func=AF.Exp,
                                  scale=np.float32(-1.0))
                    c.inc(sc.activation(out=sp_t1[i % 2][:, :], in_=sp_t1[i % 2][:, :], func=AF.Ln,
                                        bias=np.float32(1.0)), "s_sp4", 1)

            @block.sync
            def _(s):
                for i in range(64):
                    mt, n = divmod(i, 2)
                    s.wait_ge(SEM["s_dt"], i + 1)
                    s.dma_start(out=dtT[mt * 128:(mt + 1) * 128, bass.ts(n, 512)],
                                in_=dt_stage[i % 2][:, :]).then_inc(SEM["s_out"], 16)
                s.wait_ge(SEM["s_ev"], 4)
                s.dma_start(out=BCt[:, :], in_=bc_s[:, :]).then_inc(SEM["s_out"], 16)
                s.wait_ge(SEM["s_out"], 16 * 65)
    return nc


def stage_s2(ins, core, xc_all):
    t0 = 1024 * core
    xcT_t = np.ascontiguousarray(xc_all[:, t0:t0 + 1024])
    Wx = np.ascontiguousarray(ins["L0_attn_W_x"].reshape(32, 128, 160).transpose(1, 0, 2))
    Wdt = np.ascontiguousarray(ins["L0_attn_W_dt"])
    bdt = np.ascontiguousarray(ins["L0_attn_b_dt"].reshape(32, 128).T)
    return {"xcT_t": xcT_t, "Wx": Wx, "Wdt": Wdt, "bdt": bdt}





def build_s3():
    nc = bass.Bass("TRN2", target_bir_lowering=False, debug=False)
    dtT = nc.dram_tensor("dtT_s", [512, 8192], BF16, kind="ExternalInput").ap()
    xcT = nc.dram_tensor("xcT_s", [512, 8192], BF16, kind="ExternalInput").ap()
    zsT = nc.dram_tensor("zsT_s", [512, 8192], BF16, kind="ExternalInput").ap()
    BC = nc.dram_tensor("BC", [32, 8192], F32, kind="ExternalInput").ap()
    Alog = nc.dram_tensor("Alog_s", [128, 4, 16], F32, kind="ExternalInput").ap()
    Ds = nc.dram_tensor("D_s", [128, 4], F32, kind="ExternalInput").ap()
    yT = nc.dram_tensor("yT_s", [512, 8192], BF16, kind="ExternalOutput").ap()

    c = Ctx(nc)
    NCH, ND = 16, 4
    NIT = NCH * ND
    with c.es:
        bcs = c.sbuf("bcs", [32, 8192], BF16)
        a_t = c.sbuf("a_t", [128, 4, 16], F32)
        d_t = c.sbuf("d_t", [128, 4], F32)
        st = c.sbuf("st", [128, 4, 16], F32)
        brep = [c.sbuf(f"brep{i}", [128, 16, 512], BF16) for i in range(2)]
        crep = [c.sbuf(f"crep{i}", [128, 16, 512], BF16) for i in range(2)]
        dt_sl = [c.sbuf(f"dt_sl{i}", [128, 512], BF16) for i in range(2)]
        xc_sl = [c.sbuf(f"xc_sl{i}", [128, 512], BF16) for i in range(2)]
        zs_sl = [c.sbuf(f"zs_sl{i}", [128, 512], BF16) for i in range(2)]
        dA = [c.sbuf(f"dA{i}", [128, 16, 512], BF16) for i in range(2)]
        x_in = c.sbuf("x_in", [128, 16, 512], BF16)
        du = c.sbuf("du", [128, 512], BF16)
        p = [c.sbuf(f"p{i}", [128, 16, 512], BF16) for i in range(2)]
        pc = c.sbuf("pc", [128, 16, 512], BF16)
        tr = c.sbuf("tr", [128, 8, 512], BF16)
        yy = c.sbuf("yy", [128, 512], F32)
        yf = c.sbuf("yf", [128, 512], F32)
        y_stage = [c.sbuf(f"ys{i}", [128, 512], BF16) for i in range(2)]

        for s in ["s_ld", "s_bc", "s_A", "s_br0", "s_br1", "s_dA", "s_p", "s_st", "s_y"]:
            c.sem(s)
        SEM = c.sems
        st_in = PStream(c, "s_in", per=3)
        st_out = PStream(c, "s_out", per=1)

        def emit_build(g, ch):
            t0 = ch * 512
            par = ch % 2
            sb = "s_br" + str(par)
            g.wait_ge(SEM["s_bc"], 16)
            if ch >= 2:
                g.wait_ge(SEM["s_y"], 4 * (ch - 1))
            g.dma_start(out=brep[par][0:1, :, :], in_=bcs[0:16, t0:t0 + 512]).then_inc(SEM[sb], 16)
            g.dma_start(out=crep[par][0:1, :, :], in_=bcs[16:32, t0:t0 + 512]).then_inc(SEM[sb], 16)
            c.cnt[sb] += 32
            pp = 1
            while pp < 128:
                g.wait_ge(SEM[sb], c.cnt[sb])
                g.dma_start(out=brep[par][pp:2 * pp, :, :], in_=brep[par][0:pp, :, :]).then_inc(SEM[sb], 16)
                g.dma_start(out=crep[par][pp:2 * pp, :, :], in_=crep[par][0:pp, :, :]).then_inc(SEM[sb], 16)
                c.cnt[sb] += 32
                pp *= 2

        with nc.Block() as block:
            @block.sync
            def _(s):
                s.dma_start(out=a_t[:, :, :], in_=Alog[:, :, :]).then_inc(SEM["s_ld"], 16)
                s.dma_start(out=d_t[:, :], in_=Ds[:, :]).then_inc(SEM["s_ld"], 16)
                for i in range(NIT + 2):
                    if i < NIT:
                        ch, dtile = divmod(i, ND)
                        t0 = ch * 512
                        r0 = dtile * 128
                        if i >= 2:
                            s.wait_ge(SEM["s_y"], i - 1)
                        st_in.inc(s.dma_start(out=dt_sl[i % 2][:, :], in_=dtT[r0:r0 + 128, t0:t0 + 512]), i)
                        st_in.inc(s.dma_start(out=xc_sl[i % 2][:, :], in_=xcT[r0:r0 + 128, t0:t0 + 512]), i)
                        st_in.inc(s.dma_start(out=zs_sl[i % 2][:, :], in_=zsT[r0:r0 + 128, t0:t0 + 512]), i)
                    if i >= 2:
                        j = i - 2
                        ch, dtile = divmod(j, ND)
                        s.wait_ge(SEM["s_y"], j + 1)
                        st_out.inc(s.dma_start(out=yT[dtile * 128:(dtile + 1) * 128, ch * 512:(ch + 1) * 512],
                                               in_=y_stage[j % 2][:, :]), j)
                for par in range(2):
                    s.wait_ge(st_out.sems[par], c.cnt["s_out" + str(par)])

            @block.gpsimd
            def _(g):
                g.dma_start(out=bcs[:, :], in_=BC[:, :]).then_inc(SEM["s_bc"], 16)
                for dtile in range(ND):
                    c.inc(g.memset(st[:, dtile, :], 0.0), "s_st", 1)
                emit_build(g, 0)
                emit_build(g, 1)
                for i in range(NIT):
                    ch, dtile = divmod(i, ND)
                    g.wait_ge(SEM["s_p"], i + 1)
                    if ch % 8 == 7:
                        c.inc(g.memset(st[:, dtile, :], 0.0), "s_st", 1)
                    else:
                        c.inc(g.tensor_copy(st[:, dtile, :], p[i % 2][:, :, 511]), "s_st", 1)
                    if i >= 1 and (i - 1) % 4 == 0:
                        chb = (i - 1) // 4 + 2
                        if chb < NCH:
                            emit_build(g, chb)

            @block.scalar
            def _(sc):
                sc.wait_ge(SEM["s_ld"], 32)
                c.inc(sc.activation(out=a_t[:, :, :], in_=a_t[:, :, :], func=AF.Exp), "s_A", 1)
                for i in range(NIT):
                    ch, dtile = divmod(i, ND)
                    sc.wait_ge(SEM["s_A"], 2)
                    st_in.wait(sc, i)
                    if i >= 2:
                        sc.wait_ge(SEM["s_p"], i - 1)
                    last = None
                    for s_ in range(16):
                        last = sc.activation(out=dA[i % 2][:, s_, :], in_=dt_sl[i % 2][:, :],
                                             func=AF.Exp, scale=a_t[:, dtile, s_:s_ + 1])
                    c.inc(last, "s_dA", 1)

            @block.vector
            def _(v):
                v.wait_ge(SEM["s_ld"], 32)
                v.wait_ge(SEM["s_A"], 1)
                c.inc(v.tensor_scalar_mul(a_t[:, :, :], a_t[:, :, :], -1.0), "s_A", 1)
                for i in range(NIT):
                    ch, dtile = divmod(i, ND)
                    par = i % 2
                    chp = ch % 2
                    st_in.wait(v, i)
                    v.wait_ge(SEM["s_br" + str(ch % 2)], 256 * (ch // 2 + 1))
                    v.wait_ge(SEM["s_dA"], i + 1)
                    v.wait_ge(SEM["s_st"], i + 3)
                    c.fence(v, "v_o", v.tensor_mul(du[:, :], dt_sl[par][:, :], xc_sl[par][:, :]))
                    last = None
                    for s_ in range(16):
                        last = v.tensor_mul(x_in[:, s_, :], du[:, :], brep[chp][:, s_, :])
                    c.fence(v, "v_o", last)
                    last = None
                    for s_ in range(16):
                        last = v.tensor_tensor_scan(p[par][:, s_, :], dA[par][:, s_, :], x_in[:, s_, :],
                                                    st[:, dtile, s_:s_ + 1], OP.mult, OP.add)
                    c.inc(last, "s_p", 1)
                    v.wait_ge(SEM["s_p"], c.cnt["s_p"])
                    c.fence(v, "v_o", v.tensor_mul(pc[:, :, :], p[par][:, :, :], crep[chp][:, :, :]))
                    c.fence(v, "v_o", v.tensor_add(tr[:, 0:8, :], pc[:, 0:8, :], pc[:, 8:16, :]))
                    c.fence(v, "v_o", v.tensor_add(pc[:, 0:4, :], tr[:, 0:4, :], tr[:, 4:8, :]))
                    c.fence(v, "v_o", v.tensor_add(tr[:, 0:2, :], pc[:, 0:2, :], pc[:, 2:4, :]))
                    c.fence(v, "v_o", v.tensor_add(yy[:, :], tr[:, 0, :], tr[:, 1, :]))
                    c.fence(v, "v_o", v.scalar_tensor_tensor(out=yf[:, :], in0=xc_sl[par][:, :],
                                                             scalar=d_t[:, dtile:dtile + 1], in1=yy[:, :],
                                                             op0=OP.mult, op1=OP.add))
                    if i >= 2:
                        st_out.wait(v, i - 2)
                    c.inc(v.tensor_mul(y_stage[par][:, :], yf[:, :], zs_sl[par][:, :]), "s_y", 1)
    return nc


def stage_s3(ins, core, dt_all, bc_all, xc_core, zs_core):
    lo = 512 * core
    Alog = np.ascontiguousarray(ins["L0_attn_A_log"][lo:lo + 512].reshape(4, 128, 16).transpose(1, 0, 2))
    Ds = np.ascontiguousarray(ins["L0_attn_D"][lo:lo + 512].reshape(4, 128).T)
    return {"dtT_s": np.ascontiguousarray(dt_all[lo:lo + 512]),
            "xcT_s": xc_core, "zsT_s": zs_core,
            "BC": np.ascontiguousarray(bc_all), "Alog_s": Alog, "D_s": Ds}





def build_s4():
    nc = bass.Bass("TRN2", target_bir_lowering=False, debug=False)
    yT = nc.dram_tensor("yT_t", [4096, 1024], BF16, kind="ExternalInput").ap()
    xT = nc.dram_tensor("xT_t", [2048, 1024], F32, kind="ExternalInput").ap()
    Wout = nc.dram_tensor("Wout", [4096, 2048], F32, kind="ExternalInput").ap()
    lnw = nc.dram_tensor("ln1w", [128, 16], F32, kind="ExternalInput").ap()
    lnb = nc.dram_tensor("ln1b", [128, 16], F32, kind="ExternalInput").ap()
    h1T = nc.dram_tensor("h1T_t", [2048, 1024], F32, kind="ExternalOutput").ap()
    ln1T = nc.dram_tensor("ln1T_t", [2048, 1024], BF16, kind="ExternalOutput").ap()

    c = Ctx(nc)
    NMT = 16
    with c.es:
        y = c.sbuf("y", [128, 32, 1024], BF16)
        h1 = c.sbuf("h1", [128, 16, 1024], F32)
        ln1 = c.sbuf("ln1", [128, 2, 1024], BF16)
        wo_b = [c.sbuf(f"wo{i}", [128, 32, 128], BF16) for i in range(2)]
        x_sl = [c.sbuf(f"xsl{i}", [128, 1024], F32) for i in range(2)]
        lnw_t = c.sbuf("lnw_t", [128, 16], F32)
        lnb_t = c.sbuf("lnb_t", [128, 16], F32)
        bca = c.sbuf("bca", [128, 2, 1024], F32)
        sh = LNShared(c, 1024)
        ps = [c.psum(f"ps{i}", [128, 512], F32) for i in range(2)]

        for s in ["s_ld", "s_mm", "s_ev", "s_on", "v_o", "g_o"]:
            c.sem(s)
        SEM = c.sems
        st_x = PStream(c, "s_x", per=1)
        st_w = PStream(c, "s_wb", per=1)
        st_h1o = PStream(c, "s_ho", per=1)
        st_lno = PStream(c, "s_lo", per=1)

        lnA = LNEmitter(c, sh, "lnA", h1, h1, NMT, bcast=bca,
                        src_ready=lambda e, mt: e.wait_ge(SEM["s_ev"], 2 * (mt + 1)))
        lnB = LNEmitter(c, sh, "lnB", h1, ln1, NMT, prev=lnA, bcast=bca,
                        gamma=lnw_t, beta=lnb_t, dst_slots=2,
                        src_ready=lambda e, mt: e.wait_ge(c.sems["lnA_fin"], mt + 1),
                        dst_free=lambda e, mt: st_lno.wait(e, mt - 2) if mt >= 2 else None,
                        bc_free=lambda e: e.wait_ge(c.sems["lnA_fin"], NMT))

        with nc.Block() as block:
            @block.gpsimd
            def _(g):
                g.dma_start(out=y[:, :, :], in_=yT.rearrange("(kt p) t -> p kt t", p=128)).then_inc(SEM["s_ld"], 16)
                g.dma_start(out=lnw_t[:, :], in_=lnw[:, :]).then_inc(SEM["s_ld"], 16)
                g.dma_start(out=lnb_t[:, :], in_=lnb[:, :]).then_inc(SEM["s_ld"], 16)
                c.fence(g, "g_o", g.memset(sh.ones[:, :], 0.0))
                c.inc(g.tensor_scalar_add(sh.ones[:, :], sh.ones[:, :], 1.0), "s_on", 1)
                for mt in range(NMT):
                    if mt >= 2:
                        g.wait_ge(SEM["s_mm"], 2 * (mt - 1))
                    st_w.inc(g.dma_start(out=wo_b[mt % 2][:, :, :],
                                         in_=Wout[:, mt * 128:(mt + 1) * 128].rearrange("(kt p) m -> p kt m", p=128)), mt)
                lnA.emit("gpsimd", g)
                lnB.emit("gpsimd", g)

            @block.sync
            def _(s):
                for i in range(NMT):
                    if i >= 2:
                        s.wait_ge(SEM["s_ev"], 2 * (i - 1))
                    st_x.inc(s.dma_start(out=x_sl[i % 2][:, :], in_=xT[i * 128:(i + 1) * 128, :]), i)
                for mt in range(NMT):
                    s.wait_ge(c.sems["lnA_fin"], mt + 1)
                    st_h1o.inc(s.dma_start(out=h1T[mt * 128:(mt + 1) * 128, :], in_=h1[:, mt, :]), mt)
                for mt in range(NMT):
                    s.wait_ge(c.sems["lnB_fin"], mt + 1)
                    st_lno.inc(s.dma_start(out=ln1T[mt * 128:(mt + 1) * 128, :], in_=ln1[:, mt % 2, :]), mt)
                for par in range(2):
                    s.wait_ge(st_h1o.sems[par], c.cnt["s_ho" + str(par)])
                    s.wait_ge(st_lno.sems[par], c.cnt["s_lo" + str(par)])

            @block.tensor
            def _(t):
                t.wait_ge(SEM["s_ld"], 48)
                t.wait_ge(SEM["s_on"], 1)
                for mt in range(NMT):
                    st_w.wait(t, mt)
                    for n in range(2):
                        if mt >= 1:
                            t.wait_ge(SEM["s_ev"], 2 * (mt - 1) + n + 1)
                        for kt in range(32):
                            t.matmul(ps[n][:, :], wo_b[mt % 2][:, kt, :], y[:, kt, bass.ts(n, 512)],
                                     start=(kt == 0), stop=(kt == 31))
                        c.inc(t.drain(), "s_mm", 1)
                lnA.emit("tensor", t)
                lnB.emit("tensor", t)

            @block.vector
            def _(v):
                for mt in range(NMT):
                    v.wait_ge(SEM["s_mm"], 2 * mt + 1)
                    st_x.wait(v, mt)
                    c.inc(v.tensor_add(h1[:, mt, 0:512], ps[0][:, :], x_sl[mt % 2][:, 0:512]), "s_ev", 1)
                    v.wait_ge(SEM["s_mm"], 2 * mt + 2)
                    c.inc(v.tensor_add(h1[:, mt, 512:1024], ps[1][:, :], x_sl[mt % 2][:, 512:1024]), "s_ev", 1)
                lnA.emit("vector", v)
                lnB.emit("vector", v)

            @block.scalar
            def _(sc):
                lnA.emit("scalar", sc)
                lnB.emit("scalar", sc)
    return nc


def stage_s4(ins, core, y_all):
    t0 = 1024 * core
    x = ins["L0_attn_x"]
    xT_full = np.ascontiguousarray(x.reshape(-1, 2048).T)
    return {"yT_t": np.ascontiguousarray(y_all[:, t0:t0 + 1024]),
            "xT_t": np.ascontiguousarray(xT_full[:, t0:t0 + 1024]),
            "Wout": ins["L0_attn_W_out"],
            "ln1w": np.ascontiguousarray(ins["L0_ffn_ln1_w"].reshape(16, 128).T),
            "ln1b": np.ascontiguousarray(ins["L0_ffn_ln1_b"].reshape(16, 128).T)}





def build_s5():
    nc = bass.Bass("TRN2", target_bir_lowering=False, debug=False)
    ln1T = nc.dram_tensor("ln1T_t", [2048, 1024], BF16, kind="ExternalInput").ap()
    wq = nc.dram_tensor("wq", [2048, 2048], F32, kind="ExternalInput").ap()
    wk = nc.dram_tensor("wk", [2048, 2048], F32, kind="ExternalInput").ap()
    wv = nc.dram_tensor("wv", [2048, 2048], F32, kind="ExternalInput").ap()
    bq = nc.dram_tensor("bq", [128, 16], F32, kind="ExternalInput").ap()
    bk = nc.dram_tensor("bk", [128, 16], F32, kind="ExternalInput").ap()
    cosT = nc.dram_tensor("cosT", [128, 1024], F32, kind="ExternalInput").ap()
    sinT = nc.dram_tensor("sinT", [128, 1024], F32, kind="ExternalInput").ap()
    qT = nc.dram_tensor("qT_t", [2048, 1024], BF16, kind="ExternalOutput").ap()
    kT = nc.dram_tensor("kT_t", [2048, 1024], BF16, kind="ExternalOutput").ap()
    vO = nc.dram_tensor("v_t", [1024, 2048], BF16, kind="ExternalOutput").ap()

    c = Ctx(nc)
    with c.es:
        ln1 = c.sbuf("ln1", [128, 16, 1024], BF16)
        cs = c.sbuf("cs", [128, 1024], F32)
        sn = c.sbuf("sn", [128, 1024], F32)
        bq_t = c.sbuf("bq_t", [128, 16], F32)
        bk_t = c.sbuf("bk_t", [128, 16], F32)
        w_b = [c.sbuf(f"wb{i}", [128, 16, 128], BF16) for i in range(2)]
        wv_b = [c.sbuf(f"wvb{i}", [128, 16, 512], BF16) for i in range(2)]
        qf = [c.sbuf(f"qf{i}", [128, 1024], F32) for i in range(2)]
        qsw = [c.sbuf(f"qsw{i}", [128, 1024], F32) for i in range(2)]
        t1 = c.sbuf("t1", [128, 1024], F32)
        mm = c.sbuf("mm", [128, 1024], F32)
        qk_st = [c.sbuf(f"qkst{i}", [128, 1024], BF16) for i in range(2)]
        v_st = [c.sbuf(f"vst{i}", [128, 512], BF16) for i in range(2)]
        ps = [c.psum(f"ps{i}", [128, 512], F32) for i in range(2)]
        ps_v = [c.psum(f"psv{i}", [128, 512], F32) for i in range(2)]

        for s in ["s_ld", "s_mm", "s_ev", "s_vmm", "s_vev", "v_o"]:
            c.sem(s)
        SEM = c.sems
        st_w = PStream(c, "s_w", per=1)
        st_wv = PStream(c, "s_wv", per=1)
        c.sem("s_qks")
        st_qk = PStream(c, "s_qo", per=1)
        st_v = PStream(c, "s_vo", per=1)
        st_sw = PStream(c, "s_sw", per=2)

        with nc.Block() as block:
            @block.gpsimd
            def _(g):
                g.dma_start(out=ln1[:, :, :], in_=ln1T.rearrange("(kt p) t -> p kt t", p=128)).then_inc(SEM["s_ld"], 16)
                g.dma_start(out=cs[:, :], in_=cosT[:, :]).then_inc(SEM["s_ld"], 16)
                g.dma_start(out=sn[:, :], in_=sinT[:, :]).then_inc(SEM["s_ld"], 16)
                g.dma_start(out=bq_t[:, :], in_=bq[:, :]).then_inc(SEM["s_ld"], 16)
                g.dma_start(out=bk_t[:, :], in_=bk[:, :]).then_inc(SEM["s_ld"], 16)
                def emit_swap(g, j):
                    g.wait_ge(SEM["s_ev"], 2 * (j + 1))
                    if j >= 2:
                        g.wait_ge(SEM["s_qks"], j - 1)
                    g.dma_start(out=qsw[j % 2][0:64, :], in_=qf[j % 2][64:128, :]).then_inc(SEM["s_sw" + str(j % 2)], 16)
                    c.cnt["s_sw" + str(j % 2)] += 16
                    st_sw.inc(g.dma_start(out=qsw[j % 2][64:128, :], in_=qf[j % 2][0:64, :]), j)
                for i in range(34):
                    if i < 32:
                        W = wq if i < 16 else wk
                        mt = i % 16
                        if i >= 2:
                            g.wait_ge(SEM["s_mm"], 2 * (i - 1))
                        st_w.inc(g.dma_start(out=w_b[i % 2][:, :, :],
                                             in_=W[:, mt * 128:(mt + 1) * 128].rearrange("(kt p) m -> p kt m", p=128)), i)
                    if i >= 2:
                        emit_swap(g, i - 2)
                for n in range(4):
                    if n >= 2:
                        g.wait_ge(SEM["s_vmm"], 8 * (n - 1))
                    st_wv.inc(g.dma_start(out=wv_b[n % 2][:, :, :],
                                          in_=wv[:, n * 512:(n + 1) * 512].rearrange("(kt p) m -> p kt m", p=128)), n)

            @block.tensor
            def _(t):
                t.wait_ge(SEM["s_ld"], 80)
                for i in range(32):
                    st_w.wait(t, i)
                    for n in range(2):
                        if i >= 1:
                            t.wait_ge(SEM["s_ev"], 2 * (i - 1) + n + 1)
                        for kt in range(16):
                            t.matmul(ps[n][:, :], w_b[i % 2][:, kt, :], ln1[:, kt, bass.ts(n, 512)],
                                     start=(kt == 0), stop=(kt == 15))
                        c.inc(t.drain(), "s_mm", 1)
                for n in range(4):
                    st_wv.wait(t, n)
                    for tmt in range(8):
                        j = n * 8 + tmt
                        if j >= 2:
                            t.wait_ge(SEM["s_vev"], j - 1)
                        for kt in range(16):
                            t.matmul(ps_v[j % 2][:, :], ln1[:, kt, tmt * 128:(tmt + 1) * 128],
                                     wv_b[n % 2][:, kt, :], start=(kt == 0), stop=(kt == 15))
                        c.inc(t.drain(), "s_vmm", 1)

            @block.vector
            def _(v):
                for i in range(32):
                    bias = bq_t if i < 16 else bk_t
                    mt = i % 16
                    if i >= 2:
                        st_sw.wait(v, i - 2)
                    for n in range(2):
                        v.wait_ge(SEM["s_mm"], 2 * i + n + 1)
                        ins = v.tensor_single_scalar(qf[i % 2][:, bass.ts(n, 512)], ps[n][:, :],
                                                     bias[:, mt:mt + 1], op=OP.add)
                        c.inc(ins, "s_ev", 1)
                    v.wait_ge(SEM["s_ev"], c.cnt["s_ev"])
                    c.fence(v, "v_o", v.tensor_mul(t1[:, :], qf[i % 2][:, :], cs[:, :]))
                    st_sw.wait(v, i)
                    c.fence(v, "v_o", v.tensor_mul(mm[:, :], qsw[i % 2][:, :], sn[:, :]))
                    if i >= 2:
                        st_qk.wait(v, i - 2)
                    ins = v.tensor_add(qk_st[i % 2][:, :], t1[:, :], mm[:, :])
                    c.inc(ins, "s_qks", 1)
                for j in range(32):
                    v.wait_ge(SEM["s_vmm"], j + 1)
                    if j >= 2:
                        st_v.wait(v, j - 2)
                    c.inc(v.tensor_copy(v_st[j % 2][:, :], ps_v[j % 2][:, :]), "s_vev", 1)

            @block.sync
            def _(s):
                for i in range(32):
                    s.wait_ge(SEM["s_qks"], i + 1)
                    O = qT if i < 16 else kT
                    mt = i % 16
                    st_qk.inc(s.dma_start(out=O[mt * 128:(mt + 1) * 128, :], in_=qk_st[i % 2][:, :]), i)
                for j in range(32):
                    n, tmt = divmod(j, 8)
                    s.wait_ge(SEM["s_vev"], j + 1)
                    st_v.inc(s.dma_start(out=vO[tmt * 128:(tmt + 1) * 128, n * 512:(n + 1) * 512],
                                         in_=v_st[j % 2][:, :]), j)
                for par in range(2):
                    s.wait_ge(st_qk.sems[par], 16 * 16)
                    s.wait_ge(st_v.sems[par], 16 * 16)
    return nc


def stage_s5(ins, core, ln1_all):
    t0 = 1024 * core
    b = core // 4
    tw0 = t0 - b * 4096
    HD = 128
    inv = 1.0 / 10000.0 ** (np.arange(0, HD, 2) / HD)
    pos = np.arange(tw0, tw0 + 1024)
    fr = pos[:, None] * inv[None]
    emb = np.concatenate([fr, fr], -1)
    return {"ln1T_t": np.ascontiguousarray(ln1_all[:, t0:t0 + 1024]),
            "wq": ins["L0_ffn_wq"], "wk": ins["L0_ffn_wk"], "wv": ins["L0_ffn_wv"],
            "bq": np.ascontiguousarray(ins["L0_ffn_bq"].reshape(16, 128).T),
            "bk": np.ascontiguousarray(ins["L0_ffn_bk"].reshape(16, 128).T),
            "cosT": np.ascontiguousarray(np.cos(emb).T.astype(np.float32)),
            "sinT": np.ascontiguousarray((np.sin(emb).T * np.concatenate([-np.ones(64), np.ones(64)])[:, None]).astype(np.float32))}





def build_s6(L=4096, npairs=4):
    nc = bass.Bass("TRN2", target_bir_lowering=False, debug=False)
    qH = nc.dram_tensor("qT_h", [npairs, 128, L], BF16, kind="ExternalInput").ap()
    kH = nc.dram_tensor("kT_h", [npairs, 128, L], BF16, kind="ExternalInput").ap()
    vH = nc.dram_tensor("v_h", [npairs, L, 128], BF16, kind="ExternalInput").ap()
    bvH = nc.dram_tensor("bv_h", [128, npairs], F32, kind="ExternalInput").ap()
    dmask = nc.dram_tensor("dmask", [128, 4, 512], BF16, kind="ExternalInput").ap()
    ctxH = nc.dram_tensor("ctxT_h", [npairs, 128, L], BF16, kind="ExternalOutput").ap()

    NQC = L // 512
    NKT = L // 128
    c = Ctx(nc)

    def ntiles(qc):
        return 4 * (qc + 1)
    TPP = sum(ntiles(qc) for qc in range(NQC))
    seq_all = []
    gi = 0
    for p_ in range(npairs):
        for qc in range(NQC):
            qb = gi
            for kt in range(ntiles(qc)):
                seq_all.append((gi, p_, qc, kt, qb))
                gi += 1
    NT = gi
    diag_count = [0] * (NT + 1)
    for i, (g_, p_, qc, kt, qb) in enumerate(seq_all):
        diag_count[i + 1] = diag_count[i] + (1 if kt >= 4 * qc else 0)

    with c.es:
        q_t = [c.sbuf(f"q{i}", [128, L], BF16) for i in range(2)]
        k_t = [c.sbuf(f"k{i}", [128, L], BF16) for i in range(2)]
        v_t = [c.sbuf(f"v{i}", [128, NKT, 128], BF16) for i in range(2)]
        bv_t = c.sbuf("bv_t", [128, npairs], F32)
        msk = c.sbuf("msk", [128, 4, 512], BF16)
        onesb = c.sbuf("onesb", [128, 1], BF16)
        onesf = c.sbuf("onesf", [1, 128], F32)
        sq2 = c.sbuf("sq2", [128, L], BF16)
        rowq = c.sbuf("rowq", [1, L], F32)
        rowk = c.sbuf("rowk", [1, L], F32)
        kmax = c.sbuf("kmax", [1, 1], F32)
        bndB = [c.sbuf(f"bndB{i}", [128, L], F32) for i in range(2)]
        sb = c.sbuf("sb", [128, 512], F32)
        pbf = [c.sbuf(f"pbf{i}", [128, 512], BF16) for i in range(2)]
        zrow = c.sbuf("zrow", [1, 512], F32)
        zrb = c.sbuf("zrb", [128, 512], F32)
        cs1 = c.sbuf("cs1", [128, 512], F32)
        ctx_st = [c.sbuf(f"cst{i}", [128, 512], BF16) for i in range(2)]
        ps_s = [c.psum(f"pss{i}", [128, 512], F32) for i in range(2)]
        ps_c = c.psum("ps_c", [128, 512], F32)
        ps_z = c.psum("ps_z", [1, 512], F32)
        ps_zb = c.psum("ps_zb", [128, 512], F32)
        ps_r = c.psum("ps_r", [1, 512], F32)

        for s in ["s_ld", "s_on", "s_sqq", "s_sqk", "s_bmm", "s_rev", "s_km", "s_rt",
                  "s_row", "s_sc", "s_sub", "s_exp", "s_msk", "s_pv", "s_zr", "s_zb",
                  "s_zrb", "s_cev", "v_o", "se_o", "g_o"]:
            c.sem(s)
        SEM = c.sems
        st_in = PStream(c, "s_in", per=3)
        st_out = PStream(c, "s_out", per=1)
        st_bc = PStream(c, "s_bcb", per=8)

        with nc.Block() as block:
            @block.gpsimd
            def _(g):
                g.dma_start(out=bv_t[:, :], in_=bvH[:, :]).then_inc(SEM["s_ld"], 16)
                g.dma_start(out=msk[:, :, :], in_=dmask[:, :, :]).then_inc(SEM["s_ld"], 16)
                c.fence(g, "g_o", g.memset(onesb[:, :], 0.0))
                c.inc(g.tensor_scalar_add(onesb[:, :], onesb[:, :], 1.0), "s_on", 1)
                c.fence(g, "g_o", g.memset(onesf[:, :], 0.0))
                c.inc(g.tensor_scalar_add(onesf[:, :], onesf[:, :], 1.0), "s_on", 1)
                def emit_loads(p_):
                    if p_ >= 2:
                        g.wait_ge(SEM["s_cev"], (p_ - 1) * NQC)
                    st_in.inc(g.dma_start(out=q_t[p_ % 2][:, :], in_=qH[p_, :, :]), p_)
                    st_in.inc(g.dma_start(out=k_t[p_ % 2][:, :], in_=kH[p_, :, :]), p_)
                    st_in.inc(g.dma_start(out=v_t[p_ % 2][:, :, :],
                                          in_=vH[p_, :, :].rearrange("(kt p) d -> p kt d", p=128)), p_)

                def emit_bcast(p_):
                    g.wait_ge(SEM["s_row"], p_ + 1)
                    if p_ >= 2:
                        g.wait_ge(SEM["s_sc"], (p_ - 1) * TPP)
                    sb_ = "s_bcb" + str(p_ % 2)
                    g.dma_start(out=bndB[p_ % 2][0:1, :], in_=rowq[:, :]).then_inc(SEM[sb_], 16)
                    c.cnt[sb_] += 16
                    pp = 1
                    while pp < 128:
                        g.wait_ge(SEM[sb_], c.cnt[sb_])
                        g.dma_start(out=bndB[p_ % 2][pp:2 * pp, :], in_=bndB[p_ % 2][0:pp, :]).then_inc(SEM[sb_], 16)
                        c.cnt[sb_] += 16
                        pp *= 2
                emit_loads(0)
                emit_loads(1)
                for p_ in range(npairs):
                    emit_bcast(p_)
                    if p_ + 2 < npairs:
                        emit_loads(p_ + 2)

            @block.scalar
            def _(sc):
                for p_ in range(npairs):
                    st_in.wait(sc, p_)
                    if p_ >= 1:
                        sc.wait_ge(SEM["s_bmm"], 16 * p_)
                    c.inc(sc.activation(out=sq2[:, :], in_=q_t[p_ % 2][:, :], func=AF.Square), "s_sqq", 1)
                    for n in range(NQC):
                        if n == 0 and p_ >= 1:
                            st_bc.wait(sc, p_ - 1)
                        sc.wait_ge(SEM["s_bmm"], 16 * p_ + n + 1)
                        c.inc(sc.activation(out=rowq[:, bass.ts(n, 512)], in_=ps_r[:, :], func=AF.Sqrt),
                              "s_rev", 1)
                    sc.wait_ge(SEM["s_bmm"], 16 * p_ + 8)
                    c.inc(sc.activation(out=sq2[:, :], in_=k_t[p_ % 2][:, :], func=AF.Square), "s_sqk", 1)
                    for n in range(NQC):
                        if n == 0 and p_ >= 1:
                            sc.wait_ge(SEM["s_km"], p_)
                        sc.wait_ge(SEM["s_bmm"], 16 * p_ + 8 + n + 1)
                        c.inc(sc.copy(rowk[:, bass.ts(n, 512)], ps_r[:, :]), "s_rev", 1)
                    sc.wait_ge(SEM["s_km"], p_ + 1)
                    c.inc(sc.activation(out=kmax[:, :], in_=kmax[:, :], func=AF.Sqrt,
                                        scale=np.float32(1.0 / 128.0)), "s_rt", 1)
                    for j in range(TPP):
                        g2 = p_ * TPP + j
                        sc.wait_ge(SEM["s_sub"], g2 + 1)
                        if g2 >= 2:
                            sc.wait_ge(SEM["s_pv"], g2 - 1)
                        c.inc(sc.activation(out=pbf[g2 % 2][:, :], in_=sb[:, :], func=AF.Exp), "s_exp", 1)

            @block.vector
            def _(v):
                for p_ in range(npairs):
                    v.wait_ge(SEM["s_rev"], 16 * (p_ + 1))
                    c.fence(v, "v_o", v.reduce_max(kmax[:, :], rowk[:, :], axis=mybir.AxisListType.X))
                    c.inc(v.engine_nop(), "s_km", 1)
                    v.wait_ge(SEM["s_rt"], p_ + 1)
                    ins = v.tensor_single_scalar(rowq[:, :], rowq[:, :], kmax[:, 0:1], op=OP.mult)
                    c.inc(ins, "s_row", 1)
                    base = p_ * TPP
                    for qc in range(NQC):
                        qb = base + sum(ntiles(x) for x in range(qc))
                        nt = ntiles(qc)
                        for kt in range(nt):
                            g2 = qb + kt
                            v.wait_ge(SEM["s_sc"], g2 + 1)
                            if kt == 0:
                                st_bc.wait(v, p_)
                            if g2 >= 1:
                                v.wait_ge(SEM["s_exp"], g2)
                            ins = v.scalar_tensor_tensor(out=sb[:, :], in0=ps_s[g2 % 2][:, :],
                                                         scalar=float(1.0 / np.sqrt(128.0)),
                                                         in1=bndB[p_ % 2][:, qc * 512:(qc + 1) * 512],
                                                         op0=OP.mult, op1=OP.subtract)
                            c.inc(ins, "s_sub", 1)
                            if kt >= 4 * qc:
                                v.wait_ge(SEM["s_exp"], g2 + 1)
                                ins = v.tensor_mul(pbf[g2 % 2][:, :], pbf[g2 % 2][:, :],
                                                   msk[:, kt - 4 * qc, :])
                                c.inc(ins, "s_msk", 1)
                        qi = p_ * NQC + qc
                        v.wait_ge(SEM["s_pv"], qb + nt)
                        c.fence(v, "v_o", v.tensor_copy(zrow[:, :], ps_z[:, :]))
                        c.fence(v, "v_o", v.reciprocal(zrow[:, :], zrow[:, :]))
                        c.inc(v.engine_nop(), "s_zr", 1)
                        v.wait_ge(SEM["s_zb"], qi + 1)
                        c.fence(v, "v_o", v.tensor_copy(zrb[:, :], ps_zb[:, :]))
                        c.inc(v.engine_nop(), "s_zrb", 1)
                        c.fence(v, "v_o", v.tensor_mul(cs1[:, :], ps_c[:, :], zrb[:, :]))
                        if qi >= 2:
                            st_out.wait(v, qi - 2)
                        ins = v.tensor_single_scalar(ctx_st[qi % 2][:, :], cs1[:, :],
                                                     bv_t[:, p_:p_ + 1], op=OP.add)
                        c.inc(ins, "s_cev", 1)

            @block.tensor
            def _(t):
                t.wait_ge(SEM["s_ld"], 32)
                t.wait_ge(SEM["s_on"], 2)
                for p_ in range(npairs):
                    st_in.wait(t, p_)
                    t.wait_ge(SEM["s_sqq"], p_ + 1)
                    for n in range(NQC):
                        m = 16 * p_ + n
                        if m >= 1:
                            t.wait_ge(SEM["s_rev"], m)
                        t.matmul(ps_r[:, :], onesb[:, :], sq2[:, bass.ts(n, 512)], start=True, stop=True)
                        c.inc(t.drain(), "s_bmm", 1)
                    t.wait_ge(SEM["s_sqk"], p_ + 1)
                    for n in range(NQC):
                        m = 16 * p_ + 8 + n
                        t.wait_ge(SEM["s_rev"], m)
                        t.matmul(ps_r[:, :], onesb[:, :], sq2[:, bass.ts(n, 512)], start=True, stop=True)
                        c.inc(t.drain(), "s_bmm", 1)
                    pseq = [s_ for s_ in seq_all if s_[1] == p_]
                    for idx in range(len(pseq) + 1):
                        if idx < len(pseq):
                            g2, _, qc, kt, qb = pseq[idx]
                            if g2 >= 2:
                                t.wait_ge(SEM["s_sub"], g2 - 1)
                            t.matmul(ps_s[g2 % 2][:, :], k_t[p_ % 2][:, kt * 128:(kt + 1) * 128],
                                     q_t[p_ % 2][:, qc * 512:(qc + 1) * 512], start=True, stop=True)
                            c.inc(t.drain(), "s_sc", 1)
                        if idx >= 1:
                            g2, _, qc, kt, qb = pseq[idx - 1]
                            nt = ntiles(qc)
                            qi = p_ * NQC + qc
                            if kt >= 4 * qc:
                                t.wait_ge(SEM["s_msk"], diag_count[g2 + 1])
                            else:
                                t.wait_ge(SEM["s_exp"], g2 + 1)
                            if kt == 0 and qi >= 1:
                                t.wait_ge(SEM["s_cev"], qi)
                            t.matmul(ps_c[:, :], v_t[p_ % 2][:, kt, :], pbf[g2 % 2][:, :],
                                     start=(kt == 0), stop=(kt == nt - 1))
                            t.matmul(ps_z[:, :], onesb[:, :], pbf[g2 % 2][:, :],
                                     start=(kt == 0), stop=(kt == nt - 1))
                            c.inc(t.drain(), "s_pv", 1)
                            if kt == nt - 1:
                                t.wait_ge(SEM["s_zr"], qi + 1)
                                if qi >= 1:
                                    t.wait_ge(SEM["s_zrb"], qi)
                                t.matmul(ps_zb[:, :], onesf[:, :], zrow[:, :], start=True, stop=True)
                                c.inc(t.drain(), "s_zb", 1)

            @block.sync
            def _(s):
                for p_ in range(npairs):
                    for qc in range(NQC):
                        qi = p_ * NQC + qc
                        s.wait_ge(SEM["s_cev"], qi + 1)
                        st_out.inc(s.dma_start(out=ctxH[p_, :, qc * 512:(qc + 1) * 512],
                                               in_=ctx_st[qi % 2][:, :]), qi)
                for par in range(2):
                    s.wait_ge(st_out.sems[par], 16 * (npairs * NQC // 2))
    return nc


def make_dmask():
    m = np.zeros((128, 4, 512), dtype=bf16)
    for j in range(4):
        for kk in range(128):
            m[kk, j, :] = (j * 128 + kk <= np.arange(512)).astype(bf16)
    return m


def stage_s6(ins, core, q_all, k_all, v_all):
    maps = {"bv_h": np.zeros((128, 4), np.float32), "dmask": make_dmask()}
    bv = ins["L0_ffn_bv"]
    qs, ks, vs, bvs = [], [], [], []
    for i in range(4):
        pair = 4 * core + i
        b, h = pair // 16, pair % 16
        qs.append(q_all[h * 128:(h + 1) * 128, b * 4096:(b + 1) * 4096])
        ks.append(k_all[h * 128:(h + 1) * 128, b * 4096:(b + 1) * 4096])
        vs.append(v_all[b * 4096:(b + 1) * 4096, h * 128:(h + 1) * 128])
        bvs.append(bv[h * 128:(h + 1) * 128])
    maps["qT_h"] = np.ascontiguousarray(np.stack(qs))
    maps["kT_h"] = np.ascontiguousarray(np.stack(ks))
    maps["v_h"] = np.ascontiguousarray(np.stack(vs))
    maps["bv_h"] = np.ascontiguousarray(np.stack(bvs, axis=1).astype(np.float32))
    return maps





def build_s7():
    nc = bass.Bass("TRN2", target_bir_lowering=False, debug=False)
    ctxT = nc.dram_tensor("ctxT_t", [2048, 1024], BF16, kind="ExternalInput").ap()
    wo = nc.dram_tensor("wo", [2048, 2048], F32, kind="ExternalInput").ap()
    bo = nc.dram_tensor("bo", [128, 16], F32, kind="ExternalInput").ap()
    x1T = nc.dram_tensor("h1T_t", [2048, 1024], F32, kind="ExternalInput").ap()
    lnw = nc.dram_tensor("ln2w", [128, 16], F32, kind="ExternalInput").ap()
    lnb = nc.dram_tensor("ln2b", [128, 16], F32, kind="ExternalInput").ap()
    hT = nc.dram_tensor("hresT_t", [2048, 1024], F32, kind="ExternalOutput").ap()
    ln2T = nc.dram_tensor("ln2T_t", [2048, 1024], BF16, kind="ExternalOutput").ap()

    c = Ctx(nc)
    NMT = 16
    with c.es:
        ctx = c.sbuf("ctx", [128, 16, 1024], BF16)
        hres = c.sbuf("hres", [128, 16, 1024], F32)
        ln2 = c.sbuf("ln2", [128, 2, 1024], BF16)
        w_b = [c.sbuf(f"wb{i}", [128, 16, 128], BF16) for i in range(2)]
        x_sl = [c.sbuf(f"xsl{i}", [128, 1024], F32) for i in range(2)]
        bo_t = c.sbuf("bo_t", [128, 16], F32)
        lnw_t = c.sbuf("lnw_t", [128, 16], F32)
        lnb_t = c.sbuf("lnb_t", [128, 16], F32)
        bca = c.sbuf("bca", [128, 2, 1024], F32)
        sh = LNShared(c, 1024)
        ps = [c.psum(f"ps{i}", [128, 512], F32) for i in range(2)]

        for s in ["s_ld", "s_mm", "s_ev", "s_on", "v_o", "g_o", "se_o"]:
            c.sem(s)
        SEM = c.sems
        st_x = PStream(c, "s_x", per=1)
        st_w = PStream(c, "s_wb", per=1)
        st_ho = PStream(c, "s_ho", per=1)
        st_lno = PStream(c, "s_lo", per=1)

        lnB = LNEmitter(c, sh, "lnB", hres, ln2, NMT, bcast=bca,
                        gamma=lnw_t, beta=lnb_t, dst_slots=2,
                        src_ready=lambda e, mt: e.wait_ge(SEM["s_ev"], 2 * (mt + 1)),
                        dst_free=lambda e, mt: st_lno.wait(e, mt - 2) if mt >= 2 else None)

        with nc.Block() as block:
            @block.gpsimd
            def _(g):
                g.dma_start(out=ctx[:, :, :], in_=ctxT.rearrange("(kt p) t -> p kt t", p=128)).then_inc(SEM["s_ld"], 16)
                g.dma_start(out=bo_t[:, :], in_=bo[:, :]).then_inc(SEM["s_ld"], 16)
                g.dma_start(out=lnw_t[:, :], in_=lnw[:, :]).then_inc(SEM["s_ld"], 16)
                g.dma_start(out=lnb_t[:, :], in_=lnb[:, :]).then_inc(SEM["s_ld"], 16)
                c.fence(g, "g_o", g.memset(sh.ones[:, :], 0.0))
                c.inc(g.tensor_scalar_add(sh.ones[:, :], sh.ones[:, :], 1.0), "s_on", 1)
                for mt in range(NMT):
                    if mt >= 2:
                        g.wait_ge(SEM["s_mm"], 2 * (mt - 1))
                    st_w.inc(g.dma_start(out=w_b[mt % 2][:, :, :],
                                         in_=wo[:, mt * 128:(mt + 1) * 128].rearrange("(kt p) m -> p kt m", p=128)), mt)
                lnB.emit("gpsimd", g)

            @block.sync
            def _(s):
                for i in range(NMT):
                    if i >= 2:
                        s.wait_ge(SEM["s_ev"], 2 * (i - 1))
                    st_x.inc(s.dma_start(out=x_sl[i % 2][:, :], in_=x1T[i * 128:(i + 1) * 128, :]), i)
                for mt in range(NMT):
                    s.wait_ge(SEM["s_ev"], 2 * (mt + 1))
                    st_ho.inc(s.dma_start(out=hT[mt * 128:(mt + 1) * 128, :], in_=hres[:, mt, :]), mt)
                for mt in range(NMT):
                    s.wait_ge(c.sems["lnB_fin"], mt + 1)
                    st_lno.inc(s.dma_start(out=ln2T[mt * 128:(mt + 1) * 128, :], in_=ln2[:, mt % 2, :]), mt)
                for par in range(2):
                    s.wait_ge(st_ho.sems[par], 16 * 8)
                    s.wait_ge(st_lno.sems[par], 16 * 8)

            @block.tensor
            def _(t):
                t.wait_ge(SEM["s_ld"], 64)
                t.wait_ge(SEM["s_on"], 1)
                for mt in range(NMT):
                    st_w.wait(t, mt)
                    for n in range(2):
                        if mt >= 1:
                            t.wait_ge(SEM["s_ev"], 2 * (mt - 1) + n + 1)
                        for kt in range(16):
                            t.matmul(ps[n][:, :], w_b[mt % 2][:, kt, :], ctx[:, kt, bass.ts(n, 512)],
                                     start=(kt == 0), stop=(kt == 15))
                        c.inc(t.drain(), "s_mm", 1)
                lnB.emit("tensor", t)

            @block.vector
            def _(v):
                for mt in range(NMT):
                    st_x.wait(v, mt)
                    for n in range(2):
                        v.wait_ge(SEM["s_mm"], 2 * mt + n + 1)
                        ins = v.scalar_tensor_tensor(out=hres[:, mt, bass.ts(n, 512)], in0=ps[n][:, :],
                                                     scalar=bo_t[:, mt:mt + 1],
                                                     in1=x_sl[mt % 2][:, bass.ts(n, 512)],
                                                     op0=OP.add, op1=OP.add)
                        c.inc(ins, "s_ev", 1)
                lnB.emit("vector", v)

            @block.scalar
            def _(sc):
                lnB.emit("scalar", sc)
    return nc


def stage_s7(ins, core, ctx_all, h1_all):
    t0 = 1024 * core
    return {"ctxT_t": np.ascontiguousarray(ctx_all[:, t0:t0 + 1024]),
            "wo": ins["L0_ffn_wo"],
            "bo": np.ascontiguousarray(ins["L0_ffn_bo"].reshape(16, 128).T),
            "h1T_t": np.ascontiguousarray(h1_all[:, t0:t0 + 1024]),
            "ln2w": np.ascontiguousarray(ins["L0_ffn_ln2_w"].reshape(16, 128).T),
            "ln2b": np.ascontiguousarray(ins["L0_ffn_ln2_b"].reshape(16, 128).T)}





def build_s8():
    nc = bass.Bass("TRN2", target_bir_lowering=False, debug=False)
    ln2T = nc.dram_tensor("ln2T_t", [2048, 1024], BF16, kind="ExternalInput").ap()
    wg = nc.dram_tensor("w_gate", [2048, 4], F32, kind="ExternalInput").ap()
    we1 = nc.dram_tensor("we1", [4, 2048, 2048], F32, kind="ExternalInput").ap()
    we2 = nc.dram_tensor("we2", [4, 2048, 2048], F32, kind="ExternalInput").ap()
    be1 = nc.dram_tensor("be1", [128, 4, 16], F32, kind="ExternalInput").ap()
    be2 = nc.dram_tensor("be2", [128, 4, 16], F32, kind="ExternalInput").ap()
    x1T = nc.dram_tensor("h1T_t", [2048, 1024], F32, kind="ExternalInput").ap()
    hT = nc.dram_tensor("hresT_t", [2048, 1024], F32, kind="ExternalInput").ap()
    idH = nc.dram_tensor("ident", [128, 128], F32, kind="ExternalInput").ap()
    x2T = nc.dram_tensor("x2T_t", [2048, 1024], F32, kind="ExternalOutput").ap()

    c = Ctx(nc)
    NMT = 16
    NE = 4
    with c.es:
        ln2 = c.sbuf("ln2", [128, 16, 1024], BF16)
        mid = c.sbuf("mid", [128, 16, 1024], BF16)
        acc = c.sbuf("acc", [128, 16, 1024], F32)
        w_b = [c.sbuf(f"wb{i}", [128, 16, 128], BF16) for i in range(2)]
        wg_t = c.sbuf("wg_t", [128, 16, 4], BF16)
        be1_t = c.sbuf("be1_t", [128, 4, 16], F32)
        be2_t = c.sbuf("be2_t", [128, 4, 16], F32)
        gl = c.sbuf("gl", [128, 4], F32)
        nmx = c.sbuf("nmx", [128, 1], F32)
        gsum = c.sbuf("gsum", [128, 1], F32)
        gex = c.sbuf("gex", [128, 4], F32)
        gpr = c.sbuf("gpr", [128, 4], F32)
        gt1 = c.sbuf("gt1", [128, 4], F32)
        gt2 = c.sbuf("gt2", [128, 1], F32)
        combt = c.sbuf("combt", [128, 8, 4], F32)
        combp = c.sbuf("combp", [4, 1024], F32)
        combB = c.sbuf("combB", [128, 4, 1024], BF16)
        idt = c.sbuf("idt", [128, 128], F32)
        acc_tmp = c.sbuf("acc_tmp", [128, 512], F32)
        x_sl = [c.sbuf(f"xsl{i}", [128, 1024], F32) for i in range(2)]
        h_sl = [c.sbuf(f"hsl{i}", [128, 1024], F32) for i in range(2)]
        bca = c.sbuf("bca", [128, 2, 1024], F32)
        sh = LNShared(c, 1024)
        ps = [c.psum(f"ps{i}", [128, 512], F32) for i in range(2)]
        ps_g = c.psum("ps_g", [128, 4], F32)
        ps_ct = c.psum("ps_ct", [4, 128], F32)

        for s in ["s_ld", "s_on", "s_gmm", "s_gev", "s_g1", "s_g2", "s_g3", "s_g4",
                  "s_ct", "s_ctm", "s_cpe", "s_cb", "s_pmm", "s_pev0", "s_pev1", "s_r2",
                  "v_o", "g_o", "se_o"]:
            c.sem(s)
        SEM = c.sems
        st_w = PStream(c, "s_wb", per=1)
        st_x = PStream(c, "s_x", per=2)
        st_o = PStream(c, "s_o", per=1)

        lnF = LNEmitter(c, sh, "lnF", acc, acc, NMT, bcast=bca,
                        src_ready=lambda e, mt: e.wait_ge(SEM["s_r2"], mt + 1),
                        dst_free=lambda e, mt: st_o.wait(e, mt - 2) if mt >= 2 else None)

        with nc.Block() as block:
            @block.gpsimd
            def _(g):
                g.dma_start(out=ln2[:, :, :], in_=ln2T.rearrange("(kt p) t -> p kt t", p=128)).then_inc(SEM["s_ld"], 16)
                g.dma_start(out=wg_t[:, :, :], in_=wg.rearrange("(kt p) e -> p kt e", p=128)).then_inc(SEM["s_ld"], 16)
                g.dma_start(out=be1_t[:, :, :], in_=be1[:, :, :]).then_inc(SEM["s_ld"], 16)
                g.dma_start(out=be2_t[:, :, :], in_=be2[:, :, :]).then_inc(SEM["s_ld"], 16)
                g.dma_start(out=idt[:, :], in_=idH[:, :]).then_inc(SEM["s_ld"], 16)
                c.fence(g, "g_o", g.memset(sh.ones[:, :], 0.0))
                c.inc(g.tensor_scalar_add(sh.ones[:, :], sh.ones[:, :], 1.0), "s_on", 1)
                g.wait_ge(SEM["s_cpe"], 8)
                g.dma_start(out=combB[0:1, :, :], in_=combp[:, :]).then_inc(SEM["s_cb"], 16)
                c.cnt["s_cb"] += 16
                pp = 1
                while pp < 128:
                    g.wait_ge(SEM["s_cb"], c.cnt["s_cb"])
                    g.dma_start(out=combB[pp:2 * pp, :, :], in_=combB[0:pp, :, :]).then_inc(SEM["s_cb"], 16)
                    c.cnt["s_cb"] += 16
                    pp *= 2
                for i in range(2 * NE * NMT):
                    if i >= 2:
                        g.wait_ge(SEM["s_pmm"], 2 * (i - 1))
                    e, half, mt = i // 32, (i // 16) % 2, i % 16
                    W = we1 if half == 0 else we2
                    st_w.inc(g.dma_start(out=w_b[i % 2][:, :, :],
                                         in_=W[e, :, mt * 128:(mt + 1) * 128].rearrange("(kt p) m -> p kt m", p=128)), i)
                lnF.emit("gpsimd", g)

            @block.sync
            def _(s):
                for i in range(NMT):
                    if i >= 2:
                        s.wait_ge(SEM["s_r2"], i - 1)
                    st_x.inc(s.dma_start(out=x_sl[i % 2][:, :], in_=x1T[i * 128:(i + 1) * 128, :]), i)
                    st_x.inc(s.dma_start(out=h_sl[i % 2][:, :], in_=hT[i * 128:(i + 1) * 128, :]), i)
                for mt in range(NMT):
                    s.wait_ge(c.sems["lnF_fin"], mt + 1)
                    st_o.inc(s.dma_start(out=x2T[mt * 128:(mt + 1) * 128, :], in_=acc[:, mt, :]), mt)
                for par in range(2):
                    s.wait_ge(st_o.sems[par], 16 * 8)

            @block.tensor
            def _(t):
                t.wait_ge(SEM["s_ld"], 80)
                t.wait_ge(SEM["s_on"], 1)
                for tt in range(8):
                    if tt >= 1:
                        t.wait_ge(SEM["s_gev"], tt)
                    for kt in range(16):
                        t.matmul(ps_g[:, :], ln2[:, kt, tt * 128:(tt + 1) * 128], wg_t[:, kt, :],
                                 start=(kt == 0), stop=(kt == 15))
                    c.inc(t.drain(), "s_gmm", 1)
                for tt in range(8):
                    t.wait_ge(SEM["s_ct"], tt + 1)
                    if tt >= 1:
                        t.wait_ge(SEM["s_cpe"], tt)
                    t.transpose(ps_ct[:, :], combt[:, tt, :], idt[:, :])
                    c.inc(t.drain(), "s_ctm", 1)
                for i in range(2 * NE * NMT):
                    e, half, mt = i // 32, (i // 16) % 2, i % 16
                    srcT = ln2 if half == 0 else mid
                    st_w.wait(t, i)
                    if half == 1 and mt == 0:
                        t.wait_ge(SEM["s_pev0"], (e + 1) * 32)
                    for n in range(2):
                        j = 2 * i + n
                        if j >= 2:
                            jp = j - 2
                            ep, halfp, mtp, np_ = jp // 64, (jp // 32) % 2, (jp % 32) // 2, jp % 2
                            sem_p = 's_pev0' if halfp == 0 else 's_pev1'
                            t.wait_ge(SEM[sem_p], ep * 32 + mtp * 2 + np_ + 1)
                        for kt in range(16):
                            t.matmul(ps[j % 2][:, :], w_b[i % 2][:, kt, :],
                                     srcT[:, kt, bass.ts(n, 512)],
                                     start=(kt == 0), stop=(kt == 15))
                        c.inc(t.drain(), "s_pmm", 1)
                lnF.emit("tensor", t)

            @block.vector
            def _(v):
                for tt in range(8):
                    v.wait_ge(SEM["s_gmm"], tt + 1)
                    c.fence(v, "v_o", v.tensor_copy(gl[:, :], ps_g[:, :]))
                    c.inc(v.engine_nop(), "s_gev", 1)
                    c.fence(v, "v_o", v.reduce_max(nmx[:, :], gl[:, :], axis=mybir.AxisListType.X,
                                                   negate=True))
                    c.inc(v.engine_nop(), "s_g1", 1)
                    v.wait_ge(SEM["s_g2"], tt + 1)
                    c.fence(v, "v_o", v.reciprocal(gsum[:, :], gsum[:, :]))
                    c.fence(v, "v_o", v.tensor_single_scalar(gpr[:, :], gex[:, :], gsum[:, 0:1], op=OP.mult))
                    c.fence(v, "v_o", v.reduce_max(gt2[:, :], gpr[:, :], axis=mybir.AxisListType.X))
                    c.fence(v, "v_o", v.tensor_single_scalar(gt1[:, :], gpr[:, :], gt2[:, 0:1], op=OP.is_lt))
                    c.fence(v, "v_o", v.tensor_mul(gt1[:, :], gt1[:, :], gpr[:, :]))
                    c.fence(v, "v_o", v.reduce_max(gt2[:, :], gt1[:, :], axis=mybir.AxisListType.X))
                    ins = v.tensor_single_scalar(gt1[:, :], gpr[:, :], gt2[:, 0:1], op=OP.is_ge)
                    c.inc(ins, "s_g3", 1)
                    v.wait_ge(SEM["s_g4"], tt + 1)
                    c.fence(v, "v_o", v.tensor_mul(gex[:, :], gex[:, :], gt1[:, :]))
                    c.fence(v, "v_o", v.reduce_sum(gsum[:, :], gex[:, :], axis=mybir.AxisListType.X))
                    c.fence(v, "v_o", v.reciprocal(gsum[:, :], gsum[:, :]))
                    ins = v.tensor_single_scalar(combt[:, tt, :], gex[:, :], gsum[:, 0:1], op=OP.mult)
                    c.inc(ins, "s_ct", 1)
                for tt in range(8):
                    v.wait_ge(SEM["s_ctm"], tt + 1)
                    c.inc(v.tensor_copy(combp[:, tt * 128:(tt + 1) * 128], ps_ct[:, :]), "s_cpe", 1)
                v.wait_ge(SEM["s_cb"], 128)
                for e in range(NE):
                    for mt in range(NMT):
                        i = (2 * e + 1) * 16 + mt
                        for n in range(2):
                            j = 2 * i + n
                            v.wait_ge(SEM["s_pmm"], j + 1)
                            sl = bass.ts(n, 512)
                            c.fence(v, "v_o", v.scalar_tensor_tensor(
                                out=acc_tmp[:, :], in0=ps[j % 2][:, :],
                                scalar=be2_t[:, e, mt:mt + 1], in1=combB[:, e, sl],
                                op0=OP.add, op1=OP.mult))
                            if e == 0:
                                ins = v.tensor_copy(acc[:, mt, sl], acc_tmp[:, :])
                            else:
                                ins = v.tensor_add(acc[:, mt, sl], acc[:, mt, sl], acc_tmp[:, :])
                            c.inc(ins, "s_pev1", 1)
                            v.wait_ge(SEM["s_pev1"], c.cnt["s_pev1"])
                for mt in range(NMT):
                    st_x.wait(v, mt)
                    c.fence(v, "v_o", v.tensor_add(acc[:, mt, :], acc[:, mt, :], x_sl[mt % 2][:, :]))
                    ins = v.tensor_add(acc[:, mt, :], acc[:, mt, :], h_sl[mt % 2][:, :])
                    c.inc(ins, "s_r2", 1)
                lnF.emit("vector", v)

            @block.scalar
            def _(sc):
                for tt in range(8):
                    sc.wait_ge(SEM["s_g1"], tt + 1)
                    ins = sc.activation(out=gex[:, :], in_=gl[:, :], func=AF.Exp,
                                        bias=nmx[:, 0:1], accum_out=gsum[:, :])
                    c.inc(ins, "s_g2", 1)
                    sc.wait_ge(SEM["s_g3"], tt + 1)
                    c.inc(sc.activation(out=gex[:, :], in_=gpr[:, :], func=AF.Exp), "s_g4", 1)
                for e in range(NE):
                    for mt in range(NMT):
                        i = 2 * e * 16 + mt
                        for n in range(2):
                            j = 2 * i + n
                            sc.wait_ge(SEM["s_pmm"], j + 1)
                            ins = sc.activation(out=mid[:, mt, bass.ts(n, 512)], in_=ps[j % 2][:, :],
                                                func=AF.Gelu, bias=be1_t[:, e, mt:mt + 1])
                            c.inc(ins, "s_pev0", 1)
                lnF.emit("scalar", sc)
    return nc


def stage_s8(ins, core, ln2_all, h1_all, h_all):
    t0 = 1024 * core
    return {"ln2T_t": np.ascontiguousarray(ln2_all[:, t0:t0 + 1024]),
            "w_gate": ins["L0_ffn_w_gate"],
            "we1": ins["L0_ffn_we1"], "we2": ins["L0_ffn_we2"],
            "be1": np.ascontiguousarray(ins["L0_ffn_be1"].reshape(4, 16, 128).transpose(2, 0, 1)),
            "be2": np.ascontiguousarray(ins["L0_ffn_be2"].reshape(4, 16, 128).transpose(2, 0, 1)),
            "h1T_t": np.ascontiguousarray(h1_all[:, t0:t0 + 1024]),
            "hresT_t": np.ascontiguousarray(h_all[:, t0:t0 + 1024]),
            "ident": np.eye(128, dtype=np.float32)}





def build_s9():
    nc = bass.Bass("TRN2", target_bir_lowering=False, debug=False)
    x2T = nc.dram_tensor("x2T_t", [2048, 1024], F32, kind="ExternalInput").ap()
    cawT = nc.dram_tensor("cawT", [2048, 2304], F32, kind="ExternalInput").ap()
    cab = nc.dram_tensor("cab", [128, 18], F32, kind="ExternalInput").ap()
    qT = nc.dram_tensor("q1T_t", [2048, 1024], BF16, kind="ExternalOutput").ap()
    kT = nc.dram_tensor("k1T_t", [128, 1024], BF16, kind="ExternalOutput").ap()
    vO = nc.dram_tensor("v1_t", [1024, 128], BF16, kind="ExternalOutput").ap()

    c = Ctx(nc)
    with c.es:
        x2b = c.sbuf("x2b", [128, 16, 1024], BF16)
        x2f = [c.sbuf(f"x2f{i}", [128, 1024], F32) for i in range(2)]
        w_b = [c.sbuf(f"wb{i}", [128, 16, 128], BF16) for i in range(2)]
        wv_b = c.sbuf("wv_b", [128, 16, 128], BF16)
        cab_t = c.sbuf("cab_t", [128, 18], F32)
        qk_st = [c.sbuf(f"qkst{i}", [128, 1024], BF16) for i in range(2)]
        v_st = [c.sbuf(f"vst{i}", [128, 128], BF16) for i in range(2)]
        ps = [c.psum(f"ps{i}", [128, 512], F32) for i in range(2)]
        ps_v = [c.psum(f"psv{i}", [128, 128], F32) for i in range(2)]

        for s in ["s_ld", "s_cast", "s_mm", "s_ev", "s_vmm", "s_vev", "v_o"]:
            c.sem(s)
        SEM = c.sems
        st_x = PStream(c, "s_x", per=1)
        st_w = PStream(c, "s_wb", per=1)
        st_qo = PStream(c, "s_qo", per=1)
        st_vo = PStream(c, "s_vo", per=1)

        with nc.Block() as block:
            @block.sync
            def _(s):
                s.dma_start(out=cab_t[:, :], in_=cab[:, :]).then_inc(SEM["s_ld"], 16)
                for i in range(16):
                    if i >= 2:
                        s.wait_ge(SEM["s_cast"], i - 1)
                    st_x.inc(s.dma_start(out=x2f[i % 2][:, :], in_=x2T[i * 128:(i + 1) * 128, :]), i)
                for i in range(17):
                    s.wait_ge(SEM["s_ev"], i + 1)
                    if i < 16:
                        st_qo.inc(s.dma_start(out=qT[i * 128:(i + 1) * 128, :], in_=qk_st[i % 2][:, :]), i)
                    else:
                        st_qo.inc(s.dma_start(out=kT[:, :], in_=qk_st[i % 2][:, :]), i)
                for j in range(8):
                    s.wait_ge(SEM["s_vev"], j + 1)
                    st_vo.inc(s.dma_start(out=vO[j * 128:(j + 1) * 128, :], in_=v_st[j % 2][:, :]), j)
                s.wait_ge(st_qo.sems[0], 16 * 9)
                s.wait_ge(st_qo.sems[1], 16 * 8)
                for par in range(2):
                    s.wait_ge(st_vo.sems[par], 16 * 4)

            @block.gpsimd
            def _(g):
                for i in range(17):
                    if i >= 2:
                        g.wait_ge(SEM["s_mm"], 2 * (i - 1))
                    st_w.inc(g.dma_start(out=w_b[i % 2][:, :, :],
                                         in_=cawT[:, i * 128:(i + 1) * 128].rearrange("(kt p) m -> p kt m", p=128)), i)
                g.dma_start(out=wv_b[:, :, :],
                            in_=cawT[:, 2176:2304].rearrange("(kt p) m -> p kt m", p=128)).then_inc(SEM["s_ld"], 16)

            @block.vector
            def _(v):
                for i in range(16):
                    st_x.wait(v, i)
                    c.inc(v.tensor_copy(x2b[:, i, :], x2f[i % 2][:, :]), "s_cast", 1)
                for i in range(17):
                    bcol = cab_t[:, i:i + 1]
                    for n in range(2):
                        v.wait_ge(SEM["s_mm"], 2 * i + n + 1)
                        if n == 0 and i >= 2:
                            st_qo.wait(v, i - 2)
                        ins = v.tensor_single_scalar(qk_st[i % 2][:, bass.ts(n, 512)], ps[n][:, :],
                                                    bcol, op=OP.add)
                        if n == 1:
                            c.inc(ins, "s_ev", 1)
                        else:
                            c.fence(v, "v_o", ins)
                for j in range(8):
                    v.wait_ge(SEM["s_vmm"], j + 1)
                    if j >= 2:
                        st_vo.wait(v, j - 2)
                    c.inc(v.tensor_copy(v_st[j % 2][:, :], ps_v[j % 2][:, :]), "s_vev", 1)

            @block.tensor
            def _(t):
                t.wait_ge(SEM["s_cast"], 16)
                for i in range(17):
                    st_w.wait(t, i)
                    for n in range(2):
                        if i >= 1:
                            t.wait_ge(SEM["s_ev"], i)
                        for kt in range(16):
                            t.matmul(ps[n][:, :], w_b[i % 2][:, kt, :], x2b[:, kt, bass.ts(n, 512)],
                                     start=(kt == 0), stop=(kt == 15))
                        c.inc(t.drain(), "s_mm", 1)
                t.wait_ge(SEM["s_ld"], 32)
                for j in range(8):
                    if j >= 2:
                        t.wait_ge(SEM["s_vev"], j - 1)
                    for kt in range(16):
                        t.matmul(ps_v[j % 2][:, :], x2b[:, kt, j * 128:(j + 1) * 128], wv_b[:, kt, :],
                                 start=(kt == 0), stop=(kt == 15))
                    c.inc(t.drain(), "s_vmm", 1)
    return nc


def stage_s9(ins, core, x2_all):
    t0 = 1024 * core
    return {"x2T_t": np.ascontiguousarray(x2_all[:, t0:t0 + 1024]),
            "cawT": np.ascontiguousarray(ins["L1_attn_c_attn_w"].T),
            "cab": np.ascontiguousarray(ins["L1_attn_c_attn_b"].reshape(18, 128).T)}





def build_s11a():
    nc = bass.Bass("TRN2", target_bir_lowering=False, debug=False)
    ctxT = nc.dram_tensor("ctx1T_t", [2048, 1024], BF16, kind="ExternalInput").ap()
    wo = nc.dram_tensor("cpwT", [2048, 2048], F32, kind="ExternalInput").ap()
    bo = nc.dram_tensor("cpb", [128, 16], F32, kind="ExternalInput").ap()
    x2T = nc.dram_tensor("x2T_t", [2048, 1024], F32, kind="ExternalInput").ap()
    x3T = nc.dram_tensor("x3T_t", [2048, 1024], F32, kind="ExternalOutput").ap()

    c = Ctx(nc)
    NMT = 16
    with c.es:
        ctx = c.sbuf("ctx", [128, 16, 1024], BF16)
        hres = c.sbuf("hres", [128, 16, 1024], F32)
        w_b = [c.sbuf(f"wb{i}", [128, 16, 128], BF16) for i in range(2)]
        x_sl = [c.sbuf(f"xsl{i}", [128, 1024], F32) for i in range(2)]
        bo_t = c.sbuf("bo_t", [128, 16], F32)
        bca = c.sbuf("bca", [128, 2, 1024], F32)
        sh = LNShared(c, 1024)
        ps = [c.psum(f"ps{i}", [128, 512], F32) for i in range(2)]

        for s in ["s_ld", "s_mm", "s_ev", "s_on", "v_o", "g_o", "se_o"]:
            c.sem(s)
        SEM = c.sems
        st_x = PStream(c, "s_x", per=1)
        st_w = PStream(c, "s_wb", per=1)
        st_o = PStream(c, "s_o", per=1)

        lnA = LNEmitter(c, sh, "lnA", hres, hres, NMT, bcast=bca,
                        src_ready=lambda e, mt: e.wait_ge(SEM["s_ev"], 2 * (mt + 1)),
                        dst_free=lambda e, mt: st_o.wait(e, mt - 2) if mt >= 2 else None)

        with nc.Block() as block:
            @block.gpsimd
            def _(g):
                g.dma_start(out=ctx[:, :, :], in_=ctxT.rearrange("(kt p) t -> p kt t", p=128)).then_inc(SEM["s_ld"], 16)
                g.dma_start(out=bo_t[:, :], in_=bo[:, :]).then_inc(SEM["s_ld"], 16)
                c.fence(g, "g_o", g.memset(sh.ones[:, :], 0.0))
                c.inc(g.tensor_scalar_add(sh.ones[:, :], sh.ones[:, :], 1.0), "s_on", 1)
                for mt in range(NMT):
                    if mt >= 2:
                        g.wait_ge(SEM["s_mm"], 2 * (mt - 1))
                    st_w.inc(g.dma_start(out=w_b[mt % 2][:, :, :],
                                         in_=wo[:, mt * 128:(mt + 1) * 128].rearrange("(kt p) m -> p kt m", p=128)), mt)
                lnA.emit("gpsimd", g)

            @block.sync
            def _(s):
                for i in range(NMT):
                    if i >= 2:
                        s.wait_ge(SEM["s_ev"], 2 * (i - 1))
                    st_x.inc(s.dma_start(out=x_sl[i % 2][:, :], in_=x2T[i * 128:(i + 1) * 128, :]), i)
                for mt in range(NMT):
                    s.wait_ge(c.sems["lnA_fin"], mt + 1)
                    st_o.inc(s.dma_start(out=x3T[mt * 128:(mt + 1) * 128, :], in_=hres[:, mt, :]), mt)
                for par in range(2):
                    s.wait_ge(st_o.sems[par], 16 * 8)

            @block.tensor
            def _(t):
                t.wait_ge(SEM["s_ld"], 32)
                t.wait_ge(SEM["s_on"], 1)
                for mt in range(NMT):
                    st_w.wait(t, mt)
                    for n in range(2):
                        if mt >= 1:
                            t.wait_ge(SEM["s_ev"], 2 * (mt - 1) + n + 1)
                        for kt in range(16):
                            t.matmul(ps[n][:, :], w_b[mt % 2][:, kt, :], ctx[:, kt, bass.ts(n, 512)],
                                     start=(kt == 0), stop=(kt == 15))
                        c.inc(t.drain(), "s_mm", 1)
                lnA.emit("tensor", t)

            @block.vector
            def _(v):
                for mt in range(NMT):
                    st_x.wait(v, mt)
                    for n in range(2):
                        v.wait_ge(SEM["s_mm"], 2 * mt + n + 1)
                        ins = v.scalar_tensor_tensor(out=hres[:, mt, bass.ts(n, 512)], in0=ps[n][:, :],
                                                     scalar=bo_t[:, mt:mt + 1],
                                                     in1=x_sl[mt % 2][:, bass.ts(n, 512)],
                                                     op0=OP.add, op1=OP.add)
                        c.inc(ins, "s_ev", 1)
                lnA.emit("vector", v)

            @block.scalar
            def _(sc):
                lnA.emit("scalar", sc)
    return nc


def stage_s11a(ins, core, ctx_all, x2_all):
    t0 = 1024 * core
    return {"ctx1T_t": np.ascontiguousarray(ctx_all[:, t0:t0 + 1024]),
            "cpwT": np.ascontiguousarray(ins["L1_attn_c_proj_w"].T),
            "cpb": np.ascontiguousarray(ins["L1_attn_c_proj_b"].reshape(16, 128).T),
            "x2T_t": np.ascontiguousarray(x2_all[:, t0:t0 + 1024])}




LN2 = np.float32(np.log(2.0))


def build_s0():
    nc = bass.Bass("TRN2", target_bir_lowering=False, debug=False)
    gupT = nc.dram_tensor("gupT", [256, 1408], I32, kind="ExternalInput").ap()
    gus = nc.dram_tensor("gus", [128, 16, 1408], F32, kind="ExternalInput").ap()
    dnT = nc.dram_tensor("dnT", [704, 256], I32, kind="ExternalInput").ap()
    dns = nc.dram_tensor("dns", [128, 44, 256], F32, kind="ExternalInput").ap()
    shg = nc.dram_tensor("shg", [128, 1408], I32, kind="ExternalInput").ap()
    shd = nc.dram_tensor("shd", [128, 256], I32, kind="ExternalInput").ap()
    wguT = nc.dram_tensor("wguT_s", [2048, 1408], BF16, kind="ExternalOutput").ap()
    wdnT = nc.dram_tensor("wdownT_s", [5632, 256], BF16, kind="ExternalOutput").ap()

    c = Ctx(nc)
    FG, FD = 1408, 256
    with c.es:
        gus_t = c.sbuf("gus_t", [128, 16, FG], F32)
        dns_t = c.sbuf("dns_t", [128, 44, FD], F32)
        shg_t = c.sbuf("shg_t", [128, FG], I32)
        shd_t = c.sbuf("shd_t", [128, FD], I32)
        pw = [c.sbuf(f"pw{i}", [128, FG], I32) for i in range(2)]
        nb = c.sbuf("nb", [128, FG], I32)
        iw = c.sbuf("iw", [128, FG], I32)
        mF = c.sbuf("mF", [128, FG], F32)
        eF = c.sbuf("eF", [128, FG], F32)
        sF = c.sbuf("sF", [128, FG], F32)
        p2 = c.sbuf("p2", [128, FG], F32)
        t0 = c.sbuf("t0", [128, FG], F32)
        ob = [c.sbuf(f"ob{i}", [128, FG], BF16) for i in range(2)]

        for s in ["s_pw0", "s_pw1", "s_e", "s_p2", "s_dec", "s_ld", "v_o", "se_o"]:
            c.sem(s)
        SEM = c.sems
        st_o = PStream(c, "s_o", per=1)

        NB_G, NB_D = 16, 44
        blocks = [("g", b) for b in range(NB_G)] + [("d", b) for b in range(NB_D)]

        with nc.Block() as block:
            @block.sync
            def _(s):
                s.dma_start(out=gus_t[:, :, :], in_=gus[:, :, :]).then_inc(SEM["s_ld"], 16)
                s.dma_start(out=dns_t[:, :, :], in_=dns[:, :, :]).then_inc(SEM["s_ld"], 16)
                s.dma_start(out=shg_t[:, :], in_=shg[:, :]).then_inc(SEM["s_ld"], 16)
                s.dma_start(out=shd_t[:, :], in_=shd[:, :]).then_inc(SEM["s_ld"], 16)
                for bi, (kind, b) in enumerate(blocks):
                    packed = gupT if kind == "g" else dnT
                    if bi >= 2:
                        s.wait_ge(SEM["s_e"], bi - 1)
                    sm = "s_pw" + str(bi % 2)
                    for r in range(8):
                        s.dma_start(out=pw[bi % 2][16 * r:16 * (r + 1), :packed.shape[1]],
                                    in_=packed[b * 16:(b + 1) * 16, :]).then_inc(SEM[sm], 16)
                    c.cnt[sm] += 128
                for par in range(2):
                    s.wait_ge(st_o.sems[par], 16 * ((len(blocks) + 1 - par) // 2))

            @block.vector
            def _(v):
                v.wait_ge(SEM["s_ld"], 64)
                for bi, (kind, b) in enumerate(blocks):
                    F = FG if kind == "g" else FD
                    sh_t = shg_t if kind == "g" else shd_t
                    sct = gus_t[:, b, :] if kind == "g" else dns_t[:, b, :]
                    v.wait_ge(SEM["s_pw" + str(bi % 2)], 128 * (bi // 2 + 1))
                    c.fence(v, "v_o", v.tensor_tensor(nb[:, :F], pw[bi % 2][:, :F], sh_t[:, :F],
                                                      op=OP.logical_shift_right))
                    ins = v.tensor_single_scalar(nb[:, :F], nb[:, :F], 15, op=OP.bitwise_and)
                    c.fence(v, "v_o", ins)
                    c.inc(v.engine_nop(), "s_e", 1)
                    c.fence(v, "v_o", v.tensor_single_scalar(iw[:, :F], nb[:, :F], 1, op=OP.bitwise_and))
                    c.fence(v, "v_o", v.tensor_copy(mF[:, :F], iw[:, :F]))
                    c.fence(v, "v_o", v.tensor_single_scalar(iw[:, :F], nb[:, :F], 3, op=OP.logical_shift_right))
                    c.fence(v, "v_o", v.tensor_copy(sF[:, :F], iw[:, :F]))
                    c.fence(v, "v_o", v.tensor_single_scalar(iw[:, :F], nb[:, :F], 1, op=OP.logical_shift_right))
                    c.fence(v, "v_o", v.tensor_single_scalar(iw[:, :F], iw[:, :F], 3, op=OP.bitwise_and))
                    c.fence(v, "v_o", v.tensor_copy(eF[:, :F], iw[:, :F]))
                    ins = v.tensor_scalar(t0[:, :F], eF[:, :F], -1.0, 0.0, op0=OP.add, op1=OP.max)
                    c.inc(ins, "s_dec", 1)
                    v.wait_ge(SEM["s_p2"], bi + 1)
                    c.fence(v, "v_o", v.tensor_scalar(mF[:, :F], mF[:, :F], 0.5, None, op0=OP.mult))
                    c.fence(v, "v_o", v.tensor_scalar(eF[:, :F], eF[:, :F], 0.0, None, op0=OP.is_gt))
                    c.fence(v, "v_o", v.tensor_add(t0[:, :F], mF[:, :F], eF[:, :F]))
                    c.fence(v, "v_o", v.tensor_mul(t0[:, :F], t0[:, :F], p2[:, :F]))
                    c.fence(v, "v_o", v.tensor_scalar(sF[:, :F], sF[:, :F], -2.0, 1.0, op0=OP.mult, op1=OP.add))
                    c.fence(v, "v_o", v.tensor_mul(t0[:, :F], t0[:, :F], sF[:, :F]))
                    if bi >= 2:
                        st_o.wait(v, bi - 2)
                    ins = v.tensor_mul(ob[bi % 2][:, :F], t0[:, :F], sct[:, :F])
                    c.inc(ins, "s_ov", 1) if False else c.inc(ins, "s_dec", 1)

            @block.scalar
            def _(sc):
                for bi, (kind, b) in enumerate(blocks):
                    F = FG if kind == "g" else FD
                    sc.wait_ge(SEM["s_dec"], 2 * bi + 1)
                    c.inc(sc.activation(out=p2[:, :F], in_=t0[:, :F], func=AF.Exp, scale=LN2), "s_p2", 1)

            @block.gpsimd
            def _(g):
                for bi, (kind, b) in enumerate(blocks):
                    F = FG if kind == "g" else FD
                    O = wguT if kind == "g" else wdnT
                    g.wait_ge(SEM["s_dec"], 2 * (bi + 1))
                    st_o.inc(g.dma_start(
                        out=O[b * 128:(b + 1) * 128, :].rearrange("(w r) f -> r w f", w=16, r=8),
                        in_=ob[bi % 2][:, :F]), bi)
    return nc


def stage_s0(ins, core):
    o0g = 1408 * core
    o0d = 256 * core
    gup = ins["L1_ffn_gate_up_packed"]
    dnp = ins["L1_ffn_down_packed"]
    gsc = ins["L1_ffn_gate_up_scales"].astype(np.float32)
    dsc = ins["L1_ffn_down_scales"].astype(np.float32)
    gus = np.ascontiguousarray(np.broadcast_to(gsc[None, :, o0g:o0g + 1408], (128, 16, 1408)))
    dns = np.ascontiguousarray(np.broadcast_to(dsc[None, :, o0d:o0d + 256], (128, 44, 256)))
    p = np.arange(128)
    shg = np.ascontiguousarray(np.broadcast_to((4 * (p // 16)).astype(np.int32)[:, None], (128, 1408)))
    shd = np.ascontiguousarray(shg[:, :256])
    return {"gupT": np.ascontiguousarray(gup[o0g:o0g + 1408, :].T),
            "gus": gus,
            "dnT": np.ascontiguousarray(dnp[o0d:o0d + 256, :].T),
            "dns": dns, "shg": shg, "shd": shd}





def build_s11b():
    nc = bass.Bass("TRN2", target_bir_lowering=False, debug=False)
    x3T = nc.dram_tensor("x3T_t", [2048, 1024], F32, kind="ExternalInput").ap()
    wguT = nc.dram_tensor("wguT", [2048, 11264], BF16, kind="ExternalInput").ap()
    hmT = nc.dram_tensor("hmidT_t", [5632, 1024], BF16, kind="ExternalOutput").ap()

    c = Ctx(nc)
    NI = 44
    with c.es:
        x3b = c.sbuf("x3b", [128, 16, 1024], BF16)
        x3f = [c.sbuf(f"x3f{i}", [128, 1024], F32) for i in range(2)]
        wg_b = [c.sbuf(f"wgb{i}", [128, 16, 128], BF16) for i in range(2)]
        wu_b = [c.sbuf(f"wub{i}", [128, 16, 128], BF16) for i in range(2)]
        gs = c.sbuf("gs", [128, 1024], F32)
        hm_st = [c.sbuf(f"hmst{i}", [128, 1024], BF16) for i in range(2)]
        ps_g = [c.psum(f"psg{i}", [128, 512], F32) for i in range(2)]
        ps_u = [c.psum(f"psu{i}", [128, 512], F32) for i in range(2)]

        for s in ["s_ld", "s_cast", "s_gmm", "s_umm", "s_sil", "s_hm", "v_o", "se_o"]:
            c.sem(s)
        SEM = c.sems
        st_x = PStream(c, "s_x", per=1)
        st_wg = PStream(c, "s_wg", per=2)
        st_o = PStream(c, "s_o", per=1)

        with nc.Block() as block:
            @block.sync
            def _(s):
                for i in range(16):
                    if i >= 2:
                        s.wait_ge(SEM["s_cast"], i - 1)
                    st_x.inc(s.dma_start(out=x3f[i % 2][:, :], in_=x3T[i * 128:(i + 1) * 128, :]), i)
                for i in range(NI):
                    s.wait_ge(SEM["s_hm"], i + 1)
                    st_o.inc(s.dma_start(out=hmT[i * 128:(i + 1) * 128, :], in_=hm_st[i % 2][:, :]), i)
                for par in range(2):
                    s.wait_ge(st_o.sems[par], 16 * (NI // 2))

            @block.gpsimd
            def _(g):
                for i in range(NI):
                    if i >= 2:
                        g.wait_ge(SEM["s_umm"], 2 * (i - 1))
                    g.dma_start(out=wg_b[i % 2][:, :, :],
                                in_=wguT[:, i * 128:(i + 1) * 128].rearrange("(kt p) m -> p kt m", p=128)
                                ).then_inc(SEM["s_wg" + str(i % 2)], 16)
                    c.cnt["s_wg" + str(i % 2)] += 16
                    st_wg.inc(g.dma_start(out=wu_b[i % 2][:, :, :],
                                          in_=wguT[:, 5632 + i * 128:5632 + (i + 1) * 128].rearrange("(kt p) m -> p kt m", p=128)), i)

            @block.tensor
            def _(t):
                t.wait_ge(SEM["s_cast"], 16)
                for i in range(NI):
                    st_wg.wait(t, i)
                    for n in range(2):
                        if i >= 1:
                            t.wait_ge(SEM["s_hm"], i)
                        for kt in range(16):
                            t.matmul(ps_g[n][:, :], wg_b[i % 2][:, kt, :], x3b[:, kt, bass.ts(n, 512)],
                                     start=(kt == 0), stop=(kt == 15))
                        c.inc(t.drain(), "s_gmm", 1)
                        for kt in range(16):
                            t.matmul(ps_u[n][:, :], wu_b[i % 2][:, kt, :], x3b[:, kt, bass.ts(n, 512)],
                                     start=(kt == 0), stop=(kt == 15))
                        c.inc(t.drain(), "s_umm", 1)

            @block.scalar
            def _(sc):
                for i in range(NI):
                    if i >= 1:
                        sc.wait_ge(SEM["s_hm"], i)
                    for n in range(2):
                        j = 2 * i + n
                        sc.wait_ge(SEM["s_gmm"], j + 1)
                        c.inc(sc.activation(out=gs[:, bass.ts(n, 512)], in_=ps_g[n][:, :],
                                            func=AF.Silu), "s_sil", 1)

            @block.vector
            def _(v):
                for i in range(16):
                    st_x.wait(v, i)
                    c.inc(v.tensor_copy(x3b[:, i, :], x3f[i % 2][:, :]), "s_cast", 1)
                for i in range(NI):
                    for n in range(2):
                        j = 2 * i + n
                        v.wait_ge(SEM["s_sil"], j + 1)
                        v.wait_ge(SEM["s_umm"], j + 1)
                        if n == 0 and i >= 2:
                            st_o.wait(v, i - 2)
                        ins = v.tensor_mul(hm_st[i % 2][:, bass.ts(n, 512)], gs[:, bass.ts(n, 512)],
                                           ps_u[n][:, :])
                        if n == 1:
                            c.inc(ins, "s_hm", 1)
                        else:
                            c.fence(v, "v_o", ins)
    return nc


def stage_s11b(ins, core, x3_all, wgu):
    t0 = 1024 * core
    return {"x3T_t": np.ascontiguousarray(x3_all[:, t0:t0 + 1024]),
            "wguT": wgu}




EPS6 = np.float32(1e-6)


def build_s11c():
    nc = bass.Bass("TRN2", target_bir_lowering=False, debug=False)
    hmT = nc.dram_tensor("hmidT_t", [5632, 1024], BF16, kind="ExternalInput").ap()
    wdT = nc.dram_tensor("wdownT", [5632, 2048], BF16, kind="ExternalInput").ap()
    x3T = nc.dram_tensor("x3T_t", [2048, 1024], F32, kind="ExternalInput").ap()
    xoT = nc.dram_tensor("xoT_t", [2048, 1024], F32, kind="ExternalOutput").ap()

    c = Ctx(nc)
    NMT, NKT = 16, 44
    with c.es:
        hm = c.sbuf("hm", [128, NKT, 1024], BF16)
        y = c.sbuf("y", [128, 16, 1024], F32)
        w_b = [c.sbuf("wb0", [128, NKT, 128], BF16)]
        x_sl = [c.sbuf("xsl0", [128, 1024], F32)]
        rrow = c.sbuf("rrow", [1, 1024], F32)
        rB = c.sbuf("rB", [128, 1, 1024], BF16)
        bca = c.sbuf("bca", [128, 2, 1024], F32)
        sh = LNShared(c, 1024, use_tmp=False)
        ps = [c.psum(f"ps{i}", [128, 512], F32) for i in range(2)]

        for s in ["s_ld", "s_on", "s_mm", "s_ev", "s_sq", "s_smm", "s_rmm", "s_rrow",
                  "s_rb", "s_r3", "s_wL", "s_xL", "v_sq", "s_vrr", "v_o", "g_o", "se_o"]:
            c.sem(s)
        SEM = c.sems
        st_o = PStream(c, "s_o", per=1)

        lnA = LNEmitter(c, sh, "lnA", y, y, NMT, bcast=bca,
                        src_ready=lambda e, mt: e.wait_ge(SEM["s_r3"], mt + 1),
                        dst_free=lambda e, mt: st_o.wait(e, mt - 2) if mt >= 2 else None)

        with nc.Block() as block:
            @block.gpsimd
            def _(g):
                g.dma_start(out=hm[:, :, :], in_=hmT.rearrange("(kt p) t -> p kt t", p=128)).then_inc(SEM["s_ld"], 16)
                c.fence(g, "g_o", g.memset(sh.ones[:, :], 0.0))
                c.inc(g.tensor_scalar_add(sh.ones[:, :], sh.ones[:, :], 1.0), "s_on", 1)
                for mt in range(NMT):
                    if mt >= 1:
                        g.wait_ge(SEM["s_mm"], 2 * mt)
                    c.inc(g.dma_start(out=w_b[0][:, :, :],
                                      in_=wdT[:, mt * 128:(mt + 1) * 128].rearrange("(kt p) m -> p kt m", p=128)),
                          "s_wL", 16)
                g.wait_ge(SEM["s_rrow"], 1)
                g.dma_start(out=rB[0:1, :, :], in_=rrow[:, :]).then_inc(SEM["s_rb"], 16)
                c.cnt["s_rb"] += 16
                pp = 1
                while pp < 128:
                    g.wait_ge(SEM["s_rb"], c.cnt["s_rb"])
                    g.dma_start(out=rB[pp:2 * pp, :, :], in_=rB[0:pp, :, :]).then_inc(SEM["s_rb"], 16)
                    c.cnt["s_rb"] += 16
                    pp *= 2
                lnA.emit("gpsimd", g)

            @block.sync
            def _(s):
                for i in range(NMT):
                    if i >= 1:
                        s.wait_ge(SEM["s_r3"], i)
                    c.inc(s.dma_start(out=x_sl[0][:, :], in_=x3T[i * 128:(i + 1) * 128, :]), "s_xL", 16)
                for mt in range(NMT):
                    s.wait_ge(c.sems["lnA_fin"], mt + 1)
                    st_o.inc(s.dma_start(out=xoT[mt * 128:(mt + 1) * 128, :], in_=y[:, mt, :]), mt)
                for par in range(2):
                    s.wait_ge(st_o.sems[par], 16 * 8)

            @block.tensor
            def _(t):
                t.wait_ge(SEM["s_ld"], 16)
                t.wait_ge(SEM["s_on"], 1)
                def stats_mm(mt):
                    t.wait_ge(SEM["s_sq"], mt + 1)
                    for n in range(2):
                        sl = bass.ts(n, 512)
                        t.matmul(sh.ps_q[:, sl], sh.ones[:, :], sh.r2_bf[:, mt % 2, sl],
                                 start=(mt == 0), stop=(mt == NMT - 1))
                    c.inc(t.drain(), "s_smm", 1)
                for mt in range(NMT):
                    t.wait_ge(SEM["s_wL"], 16 * (mt + 1))
                    for n in range(2):
                        if mt >= 1:
                            t.wait_ge(SEM["s_ev"], 2 * (mt - 1) + n + 1)
                        for kt in range(NKT):
                            t.matmul(ps[n][:, :], w_b[0][:, kt, :], hm[:, kt, bass.ts(n, 512)],
                                     start=(kt == 0), stop=(kt == NKT - 1))
                        c.inc(t.drain(), "s_mm", 1)
                    if mt >= 1:
                        stats_mm(mt - 1)
                stats_mm(NMT - 1)
                c.inc(t.drain(), "s_rmm", 1)
                lnA.emit("tensor", t)

            @block.vector
            def _(v):
                for mt in range(NMT):
                    for n in range(2):
                        v.wait_ge(SEM["s_mm"], 2 * mt + n + 1)
                        c.inc(v.tensor_copy(y[:, mt, bass.ts(n, 512)], ps[n][:, :]), "s_ev", 1)
                    v.wait_ge(SEM["s_ev"], c.cnt["s_ev"])
                    if mt >= 2:
                        v.wait_ge(SEM["s_smm"], mt - 1)
                    c.inc(v.tensor_copy(sh.r_bf[:, mt % 2, :], y[:, mt, :]), "v_sq", 1)
                v.wait_ge(SEM["s_rmm"], 1)
                ins = v.tensor_scalar(rrow[:, :], sh.ps_q[:, :], 1.0 / 2048.0, None, op0=OP.mult)
                c.inc(ins, "s_vrr", 1)
                for mt in range(NMT):
                    v.wait_ge(SEM["s_rb"], 128)
                    v.wait_ge(SEM["s_xL"], 16 * (mt + 1))
                    c.fence(v, "v_o", v.tensor_mul(y[:, mt, :], y[:, mt, :], rB[:, 0, :]))
                    ins = v.tensor_add(y[:, mt, :], y[:, mt, :], x_sl[0][:, :])
                    c.inc(ins, "s_r3", 1)
                lnA.emit("vector", v)

            @block.scalar
            def _(sc):
                for mt in range(NMT):
                    sc.wait_ge(c.sems["v_sq"], mt + 1)
                    c.inc(sc.activation(out=sh.r2_bf[:, mt % 2, :], in_=sh.r_bf[:, mt % 2, :],
                                        func=AF.Square), "s_sq", 1)
                sc.wait_ge(c.sems["s_vrr"], 1)
                c.fence(sc, "se_o", sc.activation(out=rrow[:, :], in_=rrow[:, :], func=AF.Ln, bias=EPS6))
                ins = sc.activation(out=rrow[:, :], in_=rrow[:, :], func=AF.Exp, scale=np.float32(-0.5))
                c.inc(ins, "s_rrow", 1)
                lnA.emit("scalar", sc)
    return nc


def stage_s11c(ins, core, hm_all, wdn, x3_all):
    t0 = 1024 * core
    return {"hmidT_t": np.ascontiguousarray(hm_all[:, t0:t0 + 1024]),
            "wdownT": wdn,
            "x3T_t": np.ascontiguousarray(x3_all[:, t0:t0 + 1024])}




_NC_CACHE = {}


def _get_nc(name, builder):
    if name not in _NC_CACHE:
        _NC_CACHE[name] = builder()
    return _NC_CACHE[name]


def _run(name, builder, in_maps):
    nc = _get_nc(name, builder)
    res = run_bass_kernel_spmd(nc, in_maps, core_ids=list(range(NCORES)))
    return res.results


def _cat(results, key, axis):
    return np.concatenate([results[cc][key] for cc in range(NCORES)], axis=axis)


def kernel(**ins):
    ins = {k: np.asarray(v) for k, v in ins.items()}

    r = _run("s1", build_s1, [stage_s1(ins, cc) for cc in range(NCORES)])
    xc_all = _cat(r, "xcT", 0)
    zs_all = _cat(r, "zsT", 0)

    r = _run("s2", build_s2, [stage_s2(ins, cc, xc_all) for cc in range(NCORES)])
    dt_all = _cat(r, "dtT_t", 1)
    bc_all = _cat(r, "BCt", 1)

    r = _run("s3", build_s3, [stage_s3(ins, cc, dt_all, bc_all,
                                       np.ascontiguousarray(xc_all[512 * cc:512 * (cc + 1)]),
                                       np.ascontiguousarray(zs_all[512 * cc:512 * (cc + 1)]))
                              for cc in range(NCORES)])
    y_all = _cat(r, "yT_s", 0)

    r = _run("s4", build_s4, [stage_s4(ins, cc, y_all) for cc in range(NCORES)])
    h1_all = _cat(r, "h1T_t", 1)
    ln1_all = _cat(r, "ln1T_t", 1)

    r = _run("s5", build_s5, [stage_s5(ins, cc, ln1_all) for cc in range(NCORES)])
    q_all = _cat(r, "qT_t", 1)
    k_all = _cat(r, "kT_t", 1)
    v_all = _cat(r, "v_t", 0)

    r = _run("s6", build_s6, [stage_s6(ins, cc, q_all, k_all, v_all) for cc in range(NCORES)])
    ctx = np.zeros((2048, 8192), dtype=bf16)
    for cc in range(NCORES):
        out = r[cc]["ctxT_h"]
        for i in range(4):
            pair = 4 * cc + i
            b, h = pair // 16, pair % 16
            ctx[h * 128:(h + 1) * 128, b * 4096:(b + 1) * 4096] = out[i]

    r = _run("s7", build_s7, [stage_s7(ins, cc, ctx, h1_all) for cc in range(NCORES)])
    h_all = _cat(r, "hresT_t", 1)
    ln2_all = _cat(r, "ln2T_t", 1)

    r = _run("s8", build_s8, [stage_s8(ins, cc, ln2_all, h1_all, h_all) for cc in range(NCORES)])
    x2_all = _cat(r, "x2T_t", 1)

    r = _run("s9", build_s9, [stage_s9(ins, cc, x2_all) for cc in range(NCORES)])
    q1 = _cat(r, "q1T_t", 1)
    k1 = _cat(r, "k1T_t", 1)
    v1 = _cat(r, "v1_t", 0)

    bv1 = np.ascontiguousarray(ins["L1_attn_c_attn_b"][2176:2304].astype(np.float32))
    in_maps = []
    for cc in range(NCORES):
        qs, ks, vs = [], [], []
        for i in range(4):
            pair = 4 * cc + i
            b, h = pair // 16, pair % 16
            qs.append(q1[h * 128:(h + 1) * 128, b * 4096:(b + 1) * 4096])
            ks.append(k1[:, b * 4096:(b + 1) * 4096])
            vs.append(v1[b * 4096:(b + 1) * 4096, :])
        in_maps.append({"qT_h": np.ascontiguousarray(np.stack(qs)),
                        "kT_h": np.ascontiguousarray(np.stack(ks)),
                        "v_h": np.ascontiguousarray(np.stack(vs)),
                        "bv_h": np.ascontiguousarray(np.broadcast_to(bv1[:, None], (128, 4))),
                        "dmask": make_dmask()})
    r = _run("s6", build_s6, in_maps)
    ctx1 = np.zeros((2048, 8192), dtype=bf16)
    for cc in range(NCORES):
        out = r[cc]["ctxT_h"]
        for i in range(4):
            pair = 4 * cc + i
            b, h = pair // 16, pair % 16
            ctx1[h * 128:(h + 1) * 128, b * 4096:(b + 1) * 4096] = out[i]

    r = _run("s11a", build_s11a, [stage_s11a(ins, cc, ctx1, x2_all) for cc in range(NCORES)])
    x3_all = _cat(r, "x3T_t", 1)

    r = _run("s0", build_s0, [stage_s0(ins, cc) for cc in range(NCORES)])
    wgu = _cat(r, "wguT_s", 1)
    wdn = _cat(r, "wdownT_s", 1)

    r = _run("s11b", build_s11b, [stage_s11b(ins, cc, x3_all, wgu) for cc in range(NCORES)])
    hm = _cat(r, "hmidT_t", 1)

    r = _run("s11c", build_s11c, [stage_s11c(ins, cc, hm, wdn, x3_all) for cc in range(NCORES)])
    xo = _cat(r, "xoT_t", 1)

    return np.ascontiguousarray(xo.T).reshape(2, 4096, 2048)
